# version 04
# baseline (speedup 1.0000x reference)
import dataclasses
import os
from contextlib import ExitStack

import numpy as np
import ml_dtypes

import concourse.bass as bass
import concourse.mybir as mybir
from concourse.bass_utils import run_bass_kernel_spmd

F32 = mybir.dt.float32
BF16 = mybir.dt.bfloat16
I32 = mybir.dt.int32
ALU = mybir.AluOpType
AF = mybir.ActivationFunctionType

P = 128
T = 2048
FT = 4096
SW = 512
NS = 4
FC = 8
DM = 1024
NST = 64
HID = 4096
NE = 8
DEPTH_SCALE = 1.0 / np.sqrt(12.0)
INV_2PI = float(1.0 / (2.0 * np.pi))
TWO_PI = float(2.0 * np.pi)

bfd = np.dtype(ml_dtypes.bfloat16)

SEM_NAMES = ("LD", "WD", "WD2", "ROWD", "LDX", "LDX2", "MM", "VV", "SS", "PP", "ST", "XC", "BAR")
DATA_SEMS = ("LD", "WD", "WD2", "ROWD", "LDX", "LDX2", "MM", "VV", "SS", "PP", "ST", "XC")


class Ctx:
    def __init__(self, nc, block, sems):
        self.nc = nc
        self.block = block
        self.sems = sems
        self.cnt = {k: 0 for k in sems}

    def inc(self, ins, name, n=1):
        self.cnt[name] += n
        ins.then_inc(self.sems[name], n)
        return self.cnt[name]

    def val(self, name):
        return self.cnt[name]

    def wait(self, eng, name, value=None):
        v = self.cnt[name] if value is None else value
        assert v <= self.cnt[name], f"wait on future {name}: {v} > {self.cnt[name]}"
        if v > 0:
            eng.wait_ge(self.sems[name], v)

    def wait_fut(self, eng, name, v):
        if v > 0:
            eng.wait_ge(self.sems[name], v)


def dynoff(ap, extra):
    return dataclasses.replace(ap, offset=ap.offset + extra)


def emit_barrier(c):
    snap = {k: c.cnt[k] for k in DATA_SEMS}
    c.cnt["BAR"] += 1
    tgt = c.cnt["BAR"]

    @c.block.gpsimd
    def _(g):
        for k, v in snap.items():
            if v:
                g.wait_ge(c.sems[k], v)
        g.sem_inc(c.sems["BAR"], 1)

    for name in ("sync", "tensor", "vector", "scalar"):
        def body(eng, tgt=tgt):
            eng.wait_ge(c.sems["BAR"], tgt)
        getattr(c.block, name)(body)



def emit_ln(c, x_src, scr, out_bf, retain=None, src_is_bf=False, rows_tok=None):
    pb0, pb1, pb2, pb3 = c.pb[0], c.pb[1], c.pb[2], c.pb[3]
    xb, sq, ab = scr["xb"], scr["sq"], scr["ab"]
    rows = scr["rows"]

    if src_is_bf:
        def xb_src(fc):
            return x_src(fc)
        vv_cast = c.val("VV")
    else:
        @c.block.vector
        def _(v):
            for fc in range(FC):
                ins = v.tensor_copy(xb[:, fc, :], x_src(fc))
            c.inc(ins, "VV")
        vv_cast = c.val("VV")

        def xb_src(fc):
            return xb[:, fc, :]

    ss_sq0 = c.val("SS")
    mm_sq0 = c.val("MM")

    @c.block.scalar
    def _(s):
        c.wait(s, "VV", vv_cast)
        for fc in range(FC):
            if fc >= 2:
                c.wait_fut(s, "MM", mm_sq0 + fc - 1)
            ins = s.activation(sq[:, fc % 2, :], xb_src(fc), AF.Square)
            c.inc(ins, "SS")

    @c.block.tensor
    def _(t):
        c.wait(t, "VV", vv_cast)
        for fc in range(FC):
            t.matmul(pb0[0:1, :], c.ones_bf[0:P, 0:1], xb_src(fc),
                     start=(fc == 0), stop=(fc == FC - 1))
        for fc in range(FC):
            c.wait(t, "SS", ss_sq0 + fc + 1)
            ins = t.matmul(pb1[0:1, :], c.ones_bf[0:P, 0:1], sq[:, fc % 2, :],
                           start=(fc == 0), stop=(fc == FC - 1))
            c.inc(ins, "MM")
    mm_sums = c.val("MM")
    assert mm_sums == mm_sq0 + FC

    mean = rows[0:1, 0:SW]
    msq = rows[0:1, SW:2 * SW]
    var = rows[0:1, 2 * SW:3 * SW]
    sd = rows[0:1, SW:2 * SW]
    a_row = rows[0:1, 2 * SW:3 * SW]
    b_row = rows[0:1, 3 * SW:4 * SW]

    @c.block.vector
    def _(v):
        c.wait(v, "MM", mm_sums)
        v.tensor_scalar_mul(mean, pb0[0:1, :], 1.0 / DM)
        v.tensor_mul(msq, mean, mean)
        ins = v.scalar_tensor_tensor(var, pb1[0:1, :], 1.0 / DM, msq,
                                     ALU.mult, ALU.subtract)
        c.inc(ins, "VV")
    vv_var = c.val("VV")

    @c.block.scalar
    def _(s):
        c.wait(s, "VV", vv_var)
        ins = s.activation(sd, var, AF.Sqrt, bias=c.cvec[0:1, 1:2])
        c.inc(ins, "SS")
    ss_sd = c.val("SS")

    @c.block.vector
    def _(v):
        c.wait(v, "SS", ss_sd)
        ins = v.reciprocal(a_row, sd)
        c.inc(ins, "VV")
        c.wait(v, "VV")
        ins = v.scalar_tensor_tensor(b_row, mean, -1.0, a_row, ALU.mult, ALU.mult)
        c.inc(ins, "VV")
    vv_b = c.val("VV")

    @c.block.tensor
    def _(t):
        c.wait(t, "VV", vv_b)
        t.matmul(pb2[:, :], c.ones_f32[0:1, 0:P], a_row, start=True, stop=True)
        ins = t.matmul(pb3[:, :], c.ones_f32[0:1, 0:P], b_row, start=True, stop=True)
        c.inc(ins, "MM")
    mm_ab = c.val("MM")

    if rows_tok is not None:
        atok, btok, sp0 = rows_tok

        @c.block.tensor
        def _(t):
            for j in range(4):
                t.transpose(c.pb[5][:, j:j + 1], rows[0:1, 2 * SW + j * P:2 * SW + (j + 1) * P],
                            c.ident_f32[0:1, 0:1])
            for j in range(4):
                ins = t.transpose(c.pb[5][:, 4 + j:5 + j],
                                  rows[0:1, 3 * SW + j * P:3 * SW + (j + 1) * P],
                                  c.ident_f32[0:1, 0:1])
            c.inc(ins, "MM")
        mm_rt = c.val("MM")

        @c.block.vector
        def _(v):
            c.wait(v, "MM", mm_rt)
            v.tensor_copy(atok[:, sp0 * 4:(sp0 + 1) * 4], c.pb[5][:, 0:4])
            ins = v.tensor_copy(btok[:, sp0 * 4:(sp0 + 1) * 4], c.pb[5][:, 4:8])
            c.inc(ins, "VV")

    a_bc = ab[:, 0:SW]
    b_bc = ab[:, SW:2 * SW]

    @c.block.vector
    def _(v):
        c.wait(v, "MM", mm_ab)
        v.tensor_copy(a_bc, pb2[:, :])
        ins = v.tensor_copy(b_bc, pb3[:, :])
        c.inc(ins, "VV")
        if retain is not None:
            v.tensor_copy(retain[0], a_bc)
            ins = v.tensor_copy(retain[1], b_bc)
            c.inc(ins, "VV")
        for fc in range(FC):
            v.tensor_mul(scr["lnt"], x_src(fc), a_bc)
            ins = v.tensor_add(out_bf(fc), scr["lnt"], b_bc)
        c.inc(ins, "VV")



def emit_ssm(c, es):
    nc, blk = c.nc, c.block
    d = c.d

    logLam = es.enter_context(nc.sbuf_tensor("s_logLam", [NST, 1], F32)).ap()
    LamIm = es.enter_context(nc.sbuf_tensor("s_LamIm", [NST, 1], F32)).ap()
    logdt = es.enter_context(nc.sbuf_tensor("s_logdt", [1, 1], F32)).ap()
    dts = es.enter_context(nc.sbuf_tensor("s_dts", [1, 1], F32)).ap()
    dtc = es.enter_context(nc.sbuf_tensor("s_dtc", [P, 1], F32)).ap()
    mcol = es.enter_context(nc.sbuf_tensor("s_mcol", [P, 1], F32)).ap()
    th = es.enter_context(nc.sbuf_tensor("s_th", [NST, 1], F32)).ap()
    thm = es.enter_context(nc.sbuf_tensor("s_thm", [NST, 1], F32)).ap()
    ki = es.enter_context(nc.sbuf_tensor("s_ki", [NST, 1], I32)).ap()
    kf = es.enter_context(nc.sbuf_tensor("s_kf", [NST, 1], F32)).ap()
    hs = es.enter_context(nc.sbuf_tensor("s_hs", [NST, 1], F32)).ap()
    hc = es.enter_context(nc.sbuf_tensor("s_hc", [NST, 1], F32)).ap()
    sc1 = es.enter_context(nc.sbuf_tensor("s_sc1", [NST, 1], F32)).ap()
    cc1 = es.enter_context(nc.sbuf_tensor("s_cc1", [NST, 1], F32)).ap()
    piv = es.enter_context(nc.sbuf_tensor("s_piv", [NST, 4], F32)).ap()
    lr = es.enter_context(nc.sbuf_tensor("s_lr", [NST, 1], F32)).ap()

    BrT = es.enter_context(nc.sbuf_tensor("s_BrT", [P, FC, NST], BF16)).ap()
    BiT = es.enter_context(nc.sbuf_tensor("s_BiT", [P, FC, NST], BF16)).ap()
    CCw = es.enter_context(nc.sbuf_tensor("s_CC", [P, DM], BF16)).ap()
    WoT = es.enter_context(nc.sbuf_tensor("s_WoT", [P, FC, DM], BF16)).ap()
    Dcol = es.enter_context(nc.sbuf_tensor("s_Dcol", [P, FC], F32)).ap()
    bout = es.enter_context(nc.sbuf_tensor("s_bout", [P, FC], F32)).ap()

    TBL = es.enter_context(nc.sbuf_tensor("s_TBL", [P, FT], F32)).ap()
    TBL2 = es.enter_context(nc.sbuf_tensor("s_TBL2", [P, FT], F32)).ap()
    ct = TBL[0:NST, :]
    st = TBL2[0:NST, :]
    ct_hi = TBL2[NST:P, :]
    st_hi = TBL[NST:P, :]
    U = es.enter_context(nc.sbuf_tensor("s_U", [P, FT], F32)).ap()
    W = es.enter_context(nc.sbuf_tensor("s_W", [P, T], F32)).ap()
    scrA = es.enter_context(nc.sbuf_tensor("s_scrA", [P, T // 2], F32)).ap()
    scrB = es.enter_context(nc.sbuf_tensor("s_scrB", [P, T // 2], F32)).ap()
    Sst = es.enter_context(nc.sbuf_tensor("s_Sst", [P, T], BF16)).ap()
    ybf = es.enter_context(nc.sbuf_tensor("s_ybf", [P, FC, SW], BF16)).ap()
    afull = es.enter_context(nc.sbuf_tensor("s_afull", [P, T], F32)).ap()
    bfull = es.enter_context(nc.sbuf_tensor("s_bfull", [P, T], F32)).ap()
    xprev = es.enter_context(nc.sbuf_tensor("s_xprev", [P, FC, SW], BF16)).ap()
    xlnb = xprev
    lscr = {
        "xb": xprev,
        "sq": es.enter_context(nc.sbuf_tensor("s_lsq", [P, 2, SW], BF16)).ap(),
        "ab": es.enter_context(nc.sbuf_tensor("s_lab", [P, 2 * SW], F32)).ap(),
        "lnt": es.enter_context(nc.sbuf_tensor("s_lnt", [P, SW], F32)).ap(),
        "rows": es.enter_context(nc.sbuf_tensor("s_lrows", [1, 4 * SW], F32)).ap(),
    }
    pscr = es.enter_context(nc.sbuf_tensor("s_pscr", [P, 2, SW], F32)).ap()

    @blk.sync
    def _(s):
        c.wait(s, "LD")
        for name, tile in (("ssm_logLam", logLam), ("ssm_LamIm", LamIm),
                           ("ssm_logdt", logdt), ("ssm_D", Dcol),
                           ("ssm_bout", bout)):
            ins = s.dma_start(out=tile, in_=d[name])
            c.inc(ins, "LD", 16)
    ld_small = c.val("LD")

    @blk.sync
    def _(g):
        ins = g.dma_start(out=BrT.rearrange("p a b -> p (a b)"), in_=d["ssm_BrT"])
        c.inc(ins, "WD", 16)
        ins = g.dma_start(out=BiT.rearrange("p a b -> p (a b)"), in_=d["ssm_BiT"])
        c.inc(ins, "WD", 16)
        ins = g.dma_start(out=CCw, in_=d["ssm_CC"])
        c.inc(ins, "WD", 16)
        ins = g.dma_start(out=WoT.rearrange("p a b -> p (a b)"), in_=d["ssm_WoutT"])
        c.inc(ins, "WD", 16)
    wd_wts = c.val("WD")

    @blk.scalar
    def _(s):
        c.wait(s, "LD", ld_small)
        s.activation(dts, logdt, AF.Exp)
        ins = s.activation(lr, logLam, AF.Exp)
        c.inc(ins, "SS")
    ss_dt = c.val("SS")

    @blk.tensor
    def _(t):
        c.wait(t, "SS", ss_dt)
        ins = t.matmul(c.pb[4][:, 0:1], c.ones_f32[0:1, 0:P], dts,
                       start=True, stop=True)
        c.inc(ins, "MM")
    mm_dt = c.val("MM")

    @blk.vector
    def _(v):
        c.wait(v, "MM", mm_dt)
        ins = v.tensor_copy(dtc, c.pb[4][:, 0:1])
        c.inc(ins, "VV")
        c.wait(v, "VV")
        v.tensor_scalar(th, LamIm, dtc[0:NST, :], None, ALU.mult)
        ins = v.tensor_scalar(kf, lr, dtc[0:NST, :], None, ALU.mult)
        c.inc(ins, "VV")
    vv_th = c.val("VV")

    @blk.scalar
    def _(s):
        c.wait(s, "VV", vv_th)
        ins = s.activation(mcol[0:NST, :], kf, AF.Exp, scale=c.cvec[0:NST, 3:4])
        c.inc(ins, "SS")
    ss_m = c.val("SS")

    @blk.vector
    def _(v):
        c.wait(v, "SS", ss_m)
        v.tensor_copy(mcol[NST:P, :], mcol[0:NST, :])
        ins = v.tensor_scalar_mul(kf, th, INV_2PI)
        c.inc(ins, "VV")
        c.wait(v, "VV")
        ins = v.tensor_copy(ki, kf)
        c.inc(ins, "VV")
        c.wait(v, "VV")
        ins = v.tensor_copy(kf, ki)
        c.inc(ins, "VV")
        c.wait(v, "VV")
        ins = v.scalar_tensor_tensor(thm, kf, -TWO_PI, th, ALU.mult, ALU.add)
        c.inc(ins, "VV")
    vv_thm = c.val("VV")

    @blk.scalar
    def _(s):
        c.wait(s, "VV", vv_thm)
        s.activation(hs, thm, AF.Sin, scale=c.cvec[0:NST, 5:6])
        ins = s.activation(hc, thm, AF.Sin, scale=c.cvec[0:NST, 4:5], bias=c.cvec[0:NST, 0:1])
        c.inc(ins, "SS")
    ss_half = c.val("SS")

    @blk.vector
    def _(v):
        c.wait(v, "SS", ss_half)
        ins = v.scalar_tensor_tensor(sc1, hs, 2.0, hc, ALU.mult, ALU.mult)
        c.inc(ins, "VV")
        v.tensor_mul(cc1, hs, hs)
        c.wait(v, "VV")
        ins = v.scalar_tensor_tensor(cc1, cc1, -2.0, c.ones_f32[0:NST, 0:1],
                                     ALU.mult, ALU.add)
        c.inc(ins, "VV")
    vv_cs = c.val("VV")

    @blk.vector
    def _(v):
        c.wait(v, "VV", vv_cs)
        v.memset(st[:, 0:1], 0.0)
        v.tensor_copy(ct[:, 0:1], c.ones_f32[0:NST, 0:1])
        v.tensor_copy(ct[:, 1:2], cc1)
        ins = v.tensor_copy(st[:, 1:2], sc1)
        c.inc(ins, "VV")
        have = 2
        while have < FT:
            ext = min(have, FT - have, T // 2)
            J = have
            cJ, sJ, nsJ, tmp = piv[:, 0:1], piv[:, 1:2], piv[:, 2:3], piv[:, 3:4]
            c.wait(v, "VV")
            v.tensor_scalar(tmp, ct[:, J - 1:J], cc1, None, ALU.mult)
            ins = v.scalar_tensor_tensor(cJ, st[:, J - 1:J], sc1, tmp,
                                         ALU.mult, ALU.subtract)
            c.inc(ins, "VV")
            c.wait(v, "VV")
            ins = v.tensor_scalar_mul(cJ, cJ, -1.0)
            c.inc(ins, "VV")
            v.tensor_scalar(tmp, st[:, J - 1:J], cc1, None, ALU.mult)
            c.wait(v, "VV")
            ins = v.scalar_tensor_tensor(sJ, ct[:, J - 1:J], sc1, tmp,
                                         ALU.mult, ALU.add)
            c.inc(ins, "VV")
            c.wait(v, "VV")
            ins = v.tensor_scalar_mul(nsJ, sJ, -1.0)
            c.inc(ins, "VV")
            c.wait(v, "VV")
            v.tensor_scalar(scrA[0:NST, 0:ext], ct[:, 0:ext], cJ, None, ALU.mult)
            ins = v.scalar_tensor_tensor(ct[:, J:J + ext], st[:, 0:ext], nsJ,
                                         scrA[0:NST, 0:ext], ALU.mult, ALU.add)
            c.inc(ins, "VV")
            v.tensor_scalar(scrA[0:NST, 0:ext], st[:, 0:ext], cJ, None, ALU.mult)
            c.wait(v, "VV")
            ins = v.scalar_tensor_tensor(st[:, J:J + ext], ct[:, 0:ext], sJ,
                                         scrA[0:NST, 0:ext], ALU.mult, ALU.add)
            c.inc(ins, "VV")
            have += ext
        c.wait(v, "VV")
        v.tensor_copy(ct_hi, ct)
        ins = v.tensor_copy(st_hi, st)
        c.inc(ins, "VV")

    last_bu_mm = [0]

    def bu_strip(half, sp):
        pos = half * T + sp * SW
        if half == 0:
            @blk.sync
            def _(g):
                c.wait(g, "MM", last_bu_mm[0])
                if sp == 0:
                    c.wait(g, "LD", ld_small)
                ins = g.dma_start(
                    out=xprev, in_=d["xT_prev"][:, :, sp * SW:(sp + 1) * SW])
                c.inc(ins, "LD", 16)
            ld_strip = c.val("LD")

            @blk.vector
            def _(v):
                c.wait(v, "LD", ld_strip)
                c.wait(v, "MM", last_bu_mm[0])

            @blk.scalar
            def _(s):
                c.wait(s, "LD", ld_strip)

            @blk.tensor
            def _(t):
                c.wait(t, "LD", ld_strip)

            def x_src(fc):
                return xprev[:, fc, :]
            retain = None
            src_is_bf = True
        else:
            @blk.vector
            def _(v):
                c.wait(v, "MM", last_bu_mm[0])

            def x_src(fc):
                return c.resid[:, fc, sp * SW:(sp + 1) * SW]
            retain = (afull[:, sp * SW:(sp + 1) * SW],
                      bfull[:, sp * SW:(sp + 1) * SW])
            src_is_bf = False

        def out_bf(fc):
            return xlnb[:, fc, :]

        emit_ln(c, x_src, lscr, out_bf, retain=retain, src_is_bf=src_is_bf)
        vv_ln = c.val("VV")

        @blk.tensor
        def _(t):
            c.wait(t, "VV", vv_ln)
            if half == 0 and sp == 0:
                c.wait(t, "WD", wd_wts)
            for fc in range(FC):
                t.matmul(c.pb[4][0:NST, :], BrT[:, fc, :], xlnb[:, fc, :],
                         start=(fc == 0), stop=(fc == FC - 1))
            for fc in range(FC):
                ins = t.matmul(c.pb[5][0:NST, :], BiT[:, fc, :], xlnb[:, fc, :],
                               start=(fc == 0), stop=(fc == FC - 1))
            c.inc(ins, "MM")
        last_bu_mm[0] = c.val("MM")
        mm_bu = c.val("MM")

        @blk.vector
        def _(v):
            c.wait(v, "MM", mm_bu)
            v.tensor_scalar(U[0:NST, pos:pos + SW], c.pb[4][0:NST, :],
                            dtc[0:NST, :], None, ALU.mult)
            ins = v.tensor_scalar(U[NST:P, pos:pos + SW], c.pb[5][0:NST, :],
                                  dtc[0:NST, :], None, ALU.mult)
            c.inc(ins, "VV")

    for sp in range(NS):
        bu_strip(0, sp)
    for sp in range(NS):
        bu_strip(1, sp)
    vv_bu = c.val("VV")

    QW = T // 2
    for half in range(4):
        hsl = slice(half * QW, (half + 1) * QW)
        vv_pre = c.val("VV")

        @blk.vector
        def _(v):
            c.wait(v, "VV", vv_pre)
            v.tensor_mul(scrA[0:NST, :], ct[:, hsl], U[0:NST, hsl])
            ins = v.tensor_mul(scrA[NST:P, :], ct_hi[:, hsl], U[NST:P, hsl])
            c.inc(ins, "VV")
        vv_ra = c.val("VV")

        @blk.gpsimd
        def _(g):
            c.wait(g, "VV", vv_pre)
            g.tensor_mul(scrB[0:NST, :], st_hi[:, hsl], U[NST:P, hsl])
            ins = g.tensor_mul(scrB[NST:P, :], st[:, hsl], U[0:NST, hsl])
            c.inc(ins, "PP")
        pp_rb = c.val("PP")

        @blk.vector
        def _(v):
            c.wait(v, "PP", pp_rb)
            c.wait(v, "VV", vv_ra)
            v.tensor_add(U[0:NST, hsl], scrA[0:NST, :], scrB[0:NST, :])
            ins = v.tensor_sub(U[NST:P, hsl], scrA[NST:P, :], scrB[NST:P, :])
            c.inc(ins, "VV")
    vv_rot = c.val("VV")

    @blk.vector
    def _(v):
        c.wait(v, "VV", vv_rot)
        mbc = mcol.broadcast_to([P, T])
        ins = v.tensor_tensor_scan(W[:, :], mbc, U[:, 0:T], 0.0,
                                   ALU.mult, ALU.add)
        c.inc(ins, "VV")
        c.wait(v, "VV")
        ins = v.tensor_copy(scrA[:, 0:1], W[:, T - 1:T])
        c.inc(ins, "VV")
        c.wait(v, "VV")
        ins = v.tensor_tensor_scan(W[:, :], mbc, U[:, T:FT], scrA[:, 0:1],
                                   ALU.mult, ALU.add)
        c.inc(ins, "VV")
    vv_scan = c.val("VV")

    for dh in range(2):
        js = slice(T + dh * QW, T + (dh + 1) * QW)
        ws = slice(dh * QW, (dh + 1) * QW)
        vv_pre = c.val("VV")

        @blk.vector
        def _(v):
            c.wait(v, "VV", vv_pre)
            v.tensor_mul(scrA[0:NST, :], ct[:, js], W[0:NST, ws])
            ins = v.tensor_mul(scrA[NST:P, :], ct_hi[:, js], W[NST:P, ws])
            c.inc(ins, "VV")
        vv_da = c.val("VV")

        @blk.gpsimd
        def _(g):
            c.wait(g, "VV", vv_pre)
            g.tensor_mul(scrB[0:NST, :], st_hi[:, js], W[NST:P, ws])
            ins = g.tensor_mul(scrB[NST:P, :], st[:, js], W[0:NST, ws])
            c.inc(ins, "PP")
        pp_db = c.val("PP")

        @blk.vector
        def _(v):
            c.wait(v, "PP", pp_db)
            c.wait(v, "VV", vv_da)
            v.tensor_sub(Sst[0:NST, ws], scrA[0:NST, :], scrB[0:NST, :])
            ins = v.scalar_tensor_tensor(Sst[NST:P, ws], scrB[NST:P, :], -1.0,
                                         scrA[NST:P, :], ALU.mult, ALU.subtract)
            c.inc(ins, "VV")
    vv_sst = c.val("VV")

    mm_wout_prev = [c.val("MM")]
    for sp in range(NS):
        ssl = slice(sp * SW, (sp + 1) * SW)

        pp_x0 = c.val("PP")
        vv_yfc = None

        mm_c0 = c.val("MM")
        vv_c0 = c.val("VV")

        @blk.tensor
        def _(t):
            c.wait(t, "VV", vv_c0)
            for oc in range(FC):
                bank = c.pb[4 + (oc % 2)]
                if oc >= 2:
                    c.wait_fut(t, "VV", vv_c0 + oc - 1)
                ins = t.matmul(bank[:, :], CCw[:, oc * P:(oc + 1) * P],
                               Sst[:, ssl], start=True, stop=True)
                c.inc(ins, "MM")

        @blk.gpsimd
        def _(g):
            c.wait(g, "VV", vv_bu)
            c.wait(g, "LD", ld_small)
            for oc in range(FC):
                if oc >= 2:
                    c.wait_fut(g, "VV", vv_c0 + oc - 1)
                g.tensor_mul(pscr[:, oc % 2, :], c.resid[:, oc, ssl], afull[:, ssl])
                ins = g.tensor_add(pscr[:, oc % 2, :], pscr[:, oc % 2, :], bfull[:, ssl])
                c.inc(ins, "PP")

        @blk.vector
        def _(v):
            c.wait(v, "LD", ld_small)
            c.wait(v, "MM", mm_wout_prev[0])
            for oc in range(FC):
                c.wait(v, "MM", mm_c0 + oc + 1)
                c.wait(v, "PP", pp_x0 + oc + 1)
                ins = v.scalar_tensor_tensor(ybf[:, oc, :], pscr[:, oc % 2, :],
                                             Dcol[:, oc:oc + 1],
                                             c.pb[4 + (oc % 2)][:, :],
                                             ALU.mult, ALU.add)
                c.inc(ins, "VV")
        vv_y = c.val("VV")
        assert vv_y == vv_c0 + FC

        mm_w0 = c.val("MM")

        @blk.tensor
        def _(t):
            c.wait(t, "VV", vv_y)
            for oc in range(FC):
                bank = c.pb[4 + (oc % 2)]
                if oc >= 2:
                    c.wait_fut(t, "VV", vv_y + oc - 1)
                for kc in range(FC):
                    ins = t.matmul(bank[:, :], WoT[:, kc, oc * P:(oc + 1) * P],
                                   ybf[:, kc, :], start=(kc == 0), stop=(kc == FC - 1))
                c.inc(ins, "MM")
        mm_wout_prev[0] = c.val("MM")

        @blk.vector
        def _(v):
            for oc in range(FC):
                bank = c.pb[4 + (oc % 2)]
                c.wait(v, "MM", mm_w0 + oc + 1)
                ins = v.scalar_tensor_tensor(c.resid[:, oc, ssl], bank[:, :],
                                             bout[:, oc:oc + 1],
                                             c.resid[:, oc, ssl], ALU.add, ALU.add)
                c.inc(ins, "VV")





def emit_moe(c, es, pfx, depth_scale):
    nc, blk = c.nc, c.block
    d = c.d
    HB = 512
    NHB = HID // HB
    NBLK = NE * NHB

    XLNB = es.enter_context(nc.sbuf_tensor(pfx + "_xlnb", [P, FC, T], BF16)).ap()
    gate_sb = es.enter_context(nc.sbuf_tensor(pfx + "_gatesb", [P, FC, 8], F32)).ap()
    abtok = es.enter_context(nc.sbuf_tensor(pfx + "_abtok", [P, 2, 16], F32)).ap()
    gs_sb = es.enter_context(nc.sbuf_tensor(pfx + "_gs", [P, 8], F32)).ap()
    gs_row = es.enter_context(nc.sbuf_tensor(pfx + "_gsr", [1, 8], F32)).ap()
    W1B = es.enter_context(nc.sbuf_tensor(pfx + "_w1b", [P, 2, FC, HB], BF16)).ap()
    W3B = es.enter_context(nc.sbuf_tensor(pfx + "_w3b", [P, 2, FC, HB], BF16)).ap()
    W2B = es.enter_context(nc.sbuf_tensor(pfx + "_w2b", [P, 2, 4, 2 * HB], BF16)).ap()
    tokw_em = es.enter_context(nc.sbuf_tensor(pfx + "_tem", [8, T], F32)).ap()
    tokw_bc = es.enter_context(nc.sbuf_tensor(pfx + "_tbc", [P, T], F32)).ap()
    tokw_row = es.enter_context(nc.sbuf_tensor(pfx + "_trow", [1, 2, SW], F32)).ap()
    ltok = es.enter_context(nc.sbuf_tensor(pfx + "_ltok", [P, 16, 8], F32)).ap()
    twk = es.enter_context(nc.sbuf_tensor(pfx + "_twk", [P, 16, 8], F32)).ap()
    t2a = es.enter_context(nc.sbuf_tensor(pfx + "_t2a", [P, 16, 8], F32)).ap()
    t2b = es.enter_context(nc.sbuf_tensor(pfx + "_t2b", [P, 16, 8], F32)).ap()
    t2c = es.enter_context(nc.sbuf_tensor(pfx + "_t2c", [P, 16, 8], F32)).ap()
    cols = es.enter_context(nc.sbuf_tensor(pfx + "_cols", [P, 8, 16], F32)).ap()
    sg = es.enter_context(nc.sbuf_tensor(pfx + "_sg", [P, 2, SW], F32)).ap()
    hraw = es.enter_context(nc.sbuf_tensor(pfx + "_hraw", [P, 2, SW], BF16)).ap()
    h2 = es.enter_context(nc.sbuf_tensor(pfx + "_h2", [P, 2, 4, SW], BF16)).ap()
    lscr = {
        "xb": es.enter_context(nc.sbuf_tensor(pfx + "_lxb", [P, FC, SW], BF16)).ap(),
        "sq": es.enter_context(nc.sbuf_tensor(pfx + "_lsq", [P, 2, SW], BF16)).ap(),
        "ab": es.enter_context(nc.sbuf_tensor(pfx + "_lab", [P, 2 * SW], F32)).ap(),
        "lnt": es.enter_context(nc.sbuf_tensor(pfx + "_lnt", [P, SW], F32)).ap(),
        "rows": es.enter_context(nc.sbuf_tensor(pfx + "_lrows", [1, 4 * SW], F32)).ap(),
    }

    last_mm = [c.val("MM")]
    for sp in range(NS):
        ssl = slice(sp * SW, (sp + 1) * SW)

        @blk.vector
        def _(v, lm=last_mm[0]):
            c.wait(v, "MM", lm)

        def x_src(fc, ssl=ssl):
            return c.resid[:, fc, ssl]

        def out_bf(fc, ssl=ssl):
            return XLNB[:, fc, ssl]

        emit_ln(c, x_src, lscr, out_bf,
                rows_tok=(abtok[:, 0, :], abtok[:, 1, :], sp))
        last_mm[0] = c.val("MM")
    vv_ln = c.val("VV")

    @blk.sync
    def _(s):
        ins = s.dma_start(out=gate_sb,
                          in_=d[pfx + "_gate"].rearrange("(fc p) e -> p fc e", p=P))
        c.inc(ins, "LD", 16)
    wd_gate = c.val("LD")

    mm_lg0 = c.val("MM")

    @blk.tensor
    def _(t):
        c.wait(t, "VV", vv_ln)
        c.wait(t, "LD", wd_gate)
        for sp in range(NS):
            for fc in range(FC):
                ins = t.matmul(c.pb[4][0:8, :], gate_sb[:, fc, :],
                               c.resid[:, fc, sp * SW:(sp + 1) * SW],
                               start=(fc == 0), stop=(fc == FC - 1))
            c.inc(ins, "MM")
            if sp < NS - 1:
                c.wait_fut(t, "VV", vv_ln + sp + 1)
        for fc in range(FC):
            ins = t.matmul(c.pbx[0:1, 0:8], c.ones_f32[0:P, 0:1], gate_sb[:, fc, :],
                           start=(fc == 0), stop=(fc == FC - 1))
        c.inc(ins, "MM")
    mm_gs = c.val("MM")

    @blk.vector
    def _(v):
        for sp in range(NS):
            c.wait(v, "MM", mm_lg0 + sp + 1)
            ins = v.tensor_copy(tokw_em[:, sp * SW:(sp + 1) * SW], c.pb[4][0:8, :])
            c.inc(ins, "VV")
        c.wait(v, "MM", mm_gs)
        ins = v.tensor_copy(gs_row, c.pbx[0:1, 0:8])
        c.inc(ins, "VV")
    vv_em = c.val("VV")

    @blk.tensor
    def _(t):
        c.wait(t, "VV", vv_em)
        ins = t.matmul(c.pbx[:, 0:8], c.ones_f32[0:1, 0:P], gs_row,
                       start=True, stop=True)
        c.inc(ins, "MM")
    mm_gsb = c.val("MM")

    @blk.vector
    def _(v):
        c.wait(v, "MM", mm_gsb)
        ins = v.tensor_copy(gs_sb, c.pbx[:, 0:8])
        c.inc(ins, "VV")
    vv_em = c.val("VV")

    if os.environ.get("KDBG"):
        @blk.gpsimd
        def _(g):
            c.wait(g, "VV", vv_em)
            ins = g.dma_start(out=d["dbg"][0:8, :], in_=tokw_em)
            c.inc(ins, "ST", 16)

    mm_tr0 = c.val("MM")

    @blk.tensor
    def _(t):
        c.wait(t, "VV", vv_em)
        for tt in range(16):
            if tt >= 1:
                c.wait_fut(t, "VV", vv_em + tt)
            ins = t.transpose(c.pb[5][:, 0:8], tokw_em[0:8, tt * P:(tt + 1) * P],
                              c.ident_f32[0:8, 0:8])
            c.inc(ins, "MM")

    @blk.vector
    def _(v):
        for tt in range(16):
            c.wait(v, "MM", mm_tr0 + tt + 1)
            ins = v.tensor_copy(ltok[:, tt, :], c.pb[5][:, 0:8])
            c.inc(ins, "VV")
    vv_ltok = c.val("VV")

    def excl_prefix(v, e_in, out):
        v.tensor_scalar_mul(out[:, :, 0:1], e_in[:, :, 0:1], 0.0)
        v.tensor_copy(out[:, :, 1:8], e_in[:, :, 0:7])
        for k in range(2, 8):
            v.tensor_copy(t2c[:, :, 0:8 - k], e_in[:, :, 0:8 - k])
            v.tensor_add(out[:, :, k:8], out[:, :, k:8], t2c[:, :, 0:8 - k])

    m1 = cols[:, 0, :]
    m2 = cols[:, 1, :]
    ed = cols[:, 2, :]
    w1c = cols[:, 3, :]
    w2c = cols[:, 4, :]

    @blk.vector
    def _(v):
        c.wait(v, "VV", vv_ltok)
        abc = abtok[:, 0:1, :].rearrange("p a w -> p w a").broadcast_to([P, 16, 8])
        bbc = abtok[:, 1:2, :].rearrange("p a w -> p w a").broadcast_to([P, 16, 8])
        gbc = gs_sb.unsqueeze(1).broadcast_to([P, 16, 8])
        v.tensor_mul(t2a, ltok, abc)
        v.tensor_mul(t2b, bbc, gbc)
        ins = v.engine_nop()
        c.inc(ins, "VV")
        c.wait(v, "VV")
        ins = v.tensor_add(ltok, t2a, t2b)
        c.inc(ins, "VV")
        c.wait(v, "VV")
        v.tensor_reduce(cols[:, 0:1, :].rearrange("p a w -> p w a"), ltok,
                        mybir.AxisListType.X, ALU.max)
        ins = v.engine_nop()
        c.inc(ins, "VV")
        c.wait(v, "VV")
        m1b = cols[:, 0:1, :].rearrange("p a w -> p w a").broadcast_to([P, 16, 8])
        v.tensor_tensor(t2a, ltok, m1b, ALU.is_equal)
        excl_prefix(v, t2a, t2b)
        v.tensor_single_scalar(t2b, t2b, 0.0, ALU.is_equal)
        v.tensor_mul(t2a, t2a, t2b)
        ins = v.engine_nop()
        c.inc(ins, "VV")
        c.wait(v, "VV")
        v.tensor_copy(twk, t2a)
        v.tensor_scalar_mul(t2b, t2a, -1e30)
        v.tensor_add(t2b, t2b, ltok)
        ins = v.engine_nop()
        c.inc(ins, "VV")
        c.wait(v, "VV")
        v.tensor_reduce(cols[:, 1:2, :].rearrange("p a w -> p w a"), t2b,
                        mybir.AxisListType.X, ALU.max)
        ins = v.engine_nop()
        c.inc(ins, "VV")
        c.wait(v, "VV")
        m2b = cols[:, 1:2, :].rearrange("p a w -> p w a").broadcast_to([P, 16, 8])
        v.tensor_tensor(t2a, t2b, m2b, ALU.is_equal)
        excl_prefix(v, t2a, t2b)
        v.tensor_single_scalar(t2b, t2b, 0.0, ALU.is_equal)
        v.tensor_mul(t2a, t2a, t2b)
        ins = v.tensor_sub(cols[:, 2, :], cols[:, 1, :], cols[:, 0, :])
        c.inc(ins, "VV")
    vv_t2a = c.val("VV")

    @blk.scalar
    def _(s):
        c.wait(s, "VV", vv_t2a)
        ins = s.activation(cols[:, 2, :], cols[:, 2, :], AF.Exp)
        c.inc(ins, "SS")
    ss_ed = c.val("SS")

    @blk.vector
    def _(v):
        c.wait(v, "SS", ss_ed)
        v.tensor_scalar(cols[:, 5, :], cols[:, 2, :], 1.0, None, ALU.add)
        ins = v.engine_nop()
        c.inc(ins, "VV")
        c.wait(v, "VV")
        ins = v.reciprocal(cols[:, 6, :], cols[:, 5, :])
        c.inc(ins, "VV")
        c.wait(v, "VV")
        v.tensor_scalar_mul(cols[:, 3, :], cols[:, 6, :], float(depth_scale))
        v.tensor_mul(cols[:, 4, :], cols[:, 2, :], cols[:, 6, :])
        ins = v.engine_nop()
        c.inc(ins, "VV")
        c.wait(v, "VV")
        ins = v.tensor_scalar_mul(cols[:, 4, :], cols[:, 4, :], float(depth_scale))
        c.inc(ins, "VV")
        c.wait(v, "VV")
        w1b = cols[:, 3:4, :].rearrange("p a w -> p w a").broadcast_to([P, 16, 8])
        w2b = cols[:, 4:5, :].rearrange("p a w -> p w a").broadcast_to([P, 16, 8])
        v.tensor_mul(t2b, twk, w1b)
        v.tensor_mul(t2c, t2a, w2b)
        ins = v.engine_nop()
        c.inc(ins, "VV")
        c.wait(v, "VV")
        ins = v.tensor_add(twk, t2b, t2c)
        c.inc(ins, "VV")
    vv_twk = c.val("VV")

    mm_tb0 = c.val("MM")

    @blk.tensor
    def _(t):
        c.wait(t, "VV", vv_twk)
        for tt in range(16):
            if tt >= 1:
                c.wait_fut(t, "VV", vv_twk + tt)
            ins = t.transpose(c.pb[5][0:8, 0:P], twk[:, tt, :], c.ident_f32)
            c.inc(ins, "MM")

    @blk.vector
    def _(v):
        for tt in range(16):
            c.wait(v, "MM", mm_tb0 + tt + 1)
            ins = v.tensor_copy(tokw_em[0:8, tt * P:(tt + 1) * P], c.pb[5][0:8, 0:P])
            c.inc(ins, "VV")
    vv_tem = c.val("VV")

    if os.environ.get("KDBG"):
        @blk.gpsimd
        def _(g):
            c.wait(g, "VV", vv_tem)
            c.wait(g, "ST")
            ins = g.dma_start(out=d["dbg"][8:16, :], in_=tokw_em)
            c.inc(ins, "ST", 16)

    def emit_wdma(k, wait_mm):
        e, b = divmod(k, NHB)
        buf = k % 2
        sem = "WD" if k % 2 == 0 else "WD2"

        @blk.sync
        def _(g):
            if wait_mm:
                c.wait_fut(g, "MM", wait_mm)
            else:
                c.wait(g, "BAR")
            ins = g.dma_start(out=W1B[:, buf].rearrange("p a b -> p (a b)"),
                              in_=d[pfx + "_w1"][e, b])
            c.inc(ins, sem, 16)
            ins = g.dma_start(out=W3B[:, buf].rearrange("p a b -> p (a b)"),
                              in_=d[pfx + "_w3"][e, b])
            c.inc(ins, sem, 16)
            ins = g.dma_start(out=W2B[:, buf].rearrange("p a b -> p (a b)"),
                              in_=d[pfx + "_w2"][e, b])
            c.inc(ins, sem, 16)

    wdb = {0: c.val("WD"), 1: c.val("WD2")}
    emit_wdma(0, None)
    emit_wdma(1, None)

    mm_done = {-1: c.val("MM"), -2: c.val("MM")}
    vv_done = {-1: c.val("VV")}
    pp_done = {-1: c.val("PP")}
    ss_prev_base = c.val("SS")

    for k in range(NBLK):
        e, b = divmod(k, NHB)
        buf = k % 2

        if b == 0:
            mm_bc0 = c.val("MM")
            rowb = c.val("ROWD")

            @blk.sync
            def _(g):
                c.wait(g, "VV", vv_tem)
                if k > 0:
                    c.wait(g, "MM", mm_done[k - 1])
                for sp in range(NS):
                    if sp >= 1:
                        c.wait(g, "ROWD", rowb + 16 * sp)
                    if sp >= 2:
                        c.wait_fut(g, "MM", mm_bc0 + sp - 1)
                    ins = g.dma_start(out=tokw_row[:, sp % 2, :],
                                      in_=tokw_em[e:e + 1, sp * SW:(sp + 1) * SW])
                    c.inc(ins, "ROWD", 16)
            wd_row0 = rowb

            vv_bc0 = c.val("VV")

            @blk.tensor
            def _(t):
                c.wait(t, "VV", vv_done[k - 1])
                for sp in range(NS):
                    c.wait(t, "ROWD", wd_row0 + 16 * (sp + 1))
                    if sp >= 2:
                        c.wait_fut(t, "VV", vv_bc0 + sp - 1)
                    ins = t.matmul(c.pb[4 + (sp % 2)][:, :], c.ones_f32[0:1, 0:P],
                                   tokw_row[0:1, sp % 2, :],
                                   start=True, stop=True)
                    c.inc(ins, "MM")
            mm_bc = c.val("MM")

            @blk.vector
            def _(v):
                c.wait(v, "PP", pp_done[k - 1])
                for sp in range(NS):
                    c.wait(v, "MM", mm_bc - NS + sp + 1)
                    ins = v.tensor_copy(tokw_bc[:, sp * SW:(sp + 1) * SW],
                                        c.pb[4 + (sp % 2)][:, :])
                    c.inc(ins, "VV")

        wd_sem = "WD" if k % 2 == 0 else "WD2"
        wd_need = wdb[k % 2] + 48 * (k // 2 + 1)
        mm0 = c.val("MM")
        vv0 = c.val("VV")
        ss0 = c.val("SS")
        pp0 = c.val("PP")

        def mm_at(sp, q):
            return mm0 + sp * 16 + q

        def vv_at(sp, q):
            return vv0 + sp * 12 + q

        def ss_at(sp, q):
            return ss0 + sp * 4 + q

        def pp_at(sp, q):
            return pp0 + sp * 4 + q

        vv_prev = vv_done[k - 1]
        pp_prev = pp_done[k - 1]

        @blk.tensor
        def _(t):
            c.wait(t, wd_sem, wd_need)
            for sp in range(NS):
                ssl = slice(sp * SW, (sp + 1) * SW)
                for m in range(4):
                    q = sp * 4 + m
                    if q >= 2:
                        c.wait_fut(t, "SS", ss0 + q - 1)
                    else:
                        c.wait(t, "SS", ss0 - 1 + q)
                    for fc in range(FC):
                        ins = t.matmul(c.pb[m % 2][:, :],
                                       W1B[:, buf, fc, m * P:(m + 1) * P],
                                       XLNB[:, fc, ssl],
                                       start=(fc == 0), stop=(fc == FC - 1))
                    c.inc(ins, "MM")
                    if q >= 2:
                        qq = q - 2
                        c.wait_fut(t, "VV", vv0 + (qq // 4) * 12 + (qq % 4) + 1)
                    else:
                        c.wait(t, "VV", max(0, vv_prev - 9 + q))
                    for fc in range(FC):
                        ins = t.matmul(c.pb[2 + m % 2][:, :],
                                       W3B[:, buf, fc, m * P:(m + 1) * P],
                                       XLNB[:, fc, ssl],
                                       start=(fc == 0), stop=(fc == FC - 1))
                    c.inc(ins, "MM")
                for oc in range(FC):
                    if oc >= 2:
                        c.wait_fut(t, "VV", vv0 + sp * 12 + 4 + oc - 1)
                    elif sp >= 1:
                        c.wait_fut(t, "VV", vv0 + (sp - 1) * 12 + 4 + oc + 7)
                    elif b == 0:
                        c.wait(t, "VV", vv0)
                    else:
                        c.wait(t, "VV", max(0, vv_prev - 1 + oc))
                    c.wait_fut(t, "PP", pp_at(sp, 4))
                    for hc in range(4):
                        ins = t.matmul(c.pb[4 + (oc % 2)][:, :],
                                       W2B[:, buf, hc, oc * P:(oc + 1) * P],
                                       h2[:, sp % 2, hc, :],
                                       start=(hc == 0), stop=(hc == 3))
                    c.inc(ins, "MM")
        mm_done[k] = c.val("MM")
        assert mm_done[k] == mm0 + NS * 16

        @blk.scalar
        def _(s):
            for sp in range(NS):
                for m in range(4):
                    q = sp * 4 + m
                    c.wait_fut(s, "MM", mm_at(sp, 2 * m + 1))
                    if q >= 2:
                        qq = q - 2
                        c.wait_fut(s, "VV", vv_at(qq // 4, qq % 4) + 1)
                    else:
                        c.wait(s, "VV", max(0, vv_prev - 9 + q))
                    ins = s.activation(sg[:, m % 2, :], c.pb[m % 2][:, :], AF.Silu)
                    c.inc(ins, "SS")
        assert c.val("SS") == ss0 + NS * 4

        @blk.vector
        def _(v):
            for sp in range(NS):
                ssl = slice(sp * SW, (sp + 1) * SW)
                for m in range(4):
                    q = sp * 4 + m
                    c.wait_fut(v, "SS", ss0 + q + 1)
                    c.wait_fut(v, "MM", mm_at(sp, 2 * m + 2))
                    if q >= 2:
                        c.wait_fut(v, "PP", pp0 + q - 1)
                    else:
                        c.wait(v, "PP", max(0, pp_prev - 1 + q))
                    ins = v.tensor_mul(hraw[:, m % 2, :], sg[:, m % 2, :], c.pb[2 + m % 2][:, :])
                    c.inc(ins, "VV")
                for oc in range(FC):
                    c.wait_fut(v, "MM", mm_at(sp, 8 + oc + 1))
                    ins = v.tensor_tensor(c.resid[:, oc, ssl],
                                          c.pb[4 + (oc % 2)][:, :],
                                          c.resid[:, oc, ssl], ALU.add)
                    c.inc(ins, "VV")
        vv_done[k] = c.val("VV")
        assert vv_done[k] == vv0 + NS * 12

        @blk.gpsimd
        def _(g):
            for sp in range(NS):
                ssl = slice(sp * SW, (sp + 1) * SW)
                for m in range(4):
                    q = sp * 4 + m
                    c.wait_fut(g, "VV", vv_at(sp, m + 1))
                    if sp >= 2:
                        c.wait_fut(g, "MM", mm_at(sp - 2, 8 + 8))
                    elif k > 0:
                        c.wait(g, "MM", mm_done[k - 1])
                    ins = g.tensor_mul(h2[:, sp % 2, m, :], hraw[:, m % 2, :],
                                       tokw_bc[:, ssl])
                    c.inc(ins, "PP")
        pp_done[k] = c.val("PP")
        assert pp_done[k] == pp0 + NS * 4

        if k == 7 and os.environ.get("KDBG2"):
            @blk.gpsimd
            def _(g):
                c.wait(g, "VV", vv_done[7])
                ins = g.dma_start(out=d["dbg2"][:, T + 8 * SW:T + 12 * SW],
                                  in_=c.resid[:, 0, :])
                c.inc(ins, "ST", 16)

        if k == 0 and os.environ.get("KDBG2"):
            @blk.gpsimd
            def _(g):
                c.wait(g, "VV", vv_done[0])
                ins = g.dma_start(out=d["dbg2"][:, T + 4 * SW:T + 8 * SW],
                                  in_=c.resid[:, 0, :])
                c.inc(ins, "ST", 16)

            @blk.gpsimd
            def _(g):
                c.wait(g, "PP", pp0 + 1)
                ins = g.dma_start(out=d["dbg2"][:, 0:T], in_=tokw_bc)
                c.inc(ins, "ST", 16)
                ins = g.dma_start(out=d["dbg2"][:, T:T + SW], in_=h2[:, 0, 0, :])
                c.inc(ins, "ST", 16)
                ins = g.dma_start(out=d["dbg2"][:, T + SW:T + 2 * SW], in_=hraw[:, 0, :])
                c.inc(ins, "ST", 16)
                ins = g.dma_start(out=d["dbg2"][:, T + 2 * SW:T + 3 * SW], in_=sg[:, 0, :])
                c.inc(ins, "ST", 16)
                ins = g.dma_start(out=d["dbg2"][:, T + 3 * SW:T + 4 * SW],
                                  in_=XLNB[:, 0, 0:SW])
                c.inc(ins, "ST", 16)

        if k + 2 < NBLK:
            emit_wdma(k + 2, mm_done[k])




SWAP_MASK = [i ^ 1 for i in range(32)]


def emit_attn(c, es):
    nc, blk = c.nc, c.block
    d = c.d
    NQ = FT // SW
    c.attn_proj_mm = {}
    c.attn_oproj_mm = {}

    ropeC = es.enter_context(nc.sbuf_tensor("a_ropec", [P, FT], BF16)).ap()
    ropeS = es.enter_context(nc.sbuf_tensor("a_ropes", [P, FT], BF16)).ap()
    masks = es.enter_context(nc.sbuf_tensor("a_masks", [P, 4, SW], BF16)).ap()
    wq = es.enter_context(nc.sbuf_tensor("a_wq", [P, FC, P], BF16)).ap()
    wk = es.enter_context(nc.sbuf_tensor("a_wk", [P, FC, P], BF16)).ap()
    wv = es.enter_context(nc.sbuf_tensor("a_wv", [P, FC, P], BF16)).ap()
    xps = es.enter_context(nc.sbuf_tensor("a_xps", [P, 2, FC, SW], BF16)).ap()
    xat = xps
    QT = es.enter_context(nc.sbuf_tensor("a_qt", [P, FT], BF16)).ap()
    KT = es.enter_context(nc.sbuf_tensor("a_kt", [P, FT], BF16)).ap()
    Vtok = es.enter_context(nc.sbuf_tensor("a_vtok", [P, 32, 130], BF16)).ap()
    OT = es.enter_context(nc.sbuf_tensor("a_ot", [P, FT], BF16)).ap()
    qr = es.enter_context(nc.sbuf_tensor("a_qr", [P, SW], F32)).ap()
    qsh = es.enter_context(nc.sbuf_tensor("a_qsh", [P, SW], F32)).ap()
    rope_t = es.enter_context(nc.sbuf_tensor("a_ropet", [P, 2, SW], F32)).ap()
    vsb = es.enter_context(nc.sbuf_tensor("a_vsb", [P, SW], BF16)).ap()
    qsq = es.enter_context(nc.sbuf_tensor("a_qsq", [P, SW], BF16)).ap()
    PT = es.enter_context(nc.sbuf_tensor("a_pt", [P, 8, SW], BF16)).ap()
    mx = es.enter_context(nc.sbuf_tensor("a_mx", [1, 2, 2, NQ], F32)).ap()
    rowsc = es.enter_context(nc.sbuf_tensor("a_rows", [1, 8], F32)).ap()
    negB = es.enter_context(nc.sbuf_tensor("a_negb", [P, 2], F32)).ap()
    rb = es.enter_context(nc.sbuf_tensor("a_rb", [NST, SW], F32)).ap()
    rrow = es.enter_context(nc.sbuf_tensor("a_rrow", [1, 2, SW], F32)).ap()
    tok0 = es.enter_context(nc.sbuf_tensor("a_tok0", [1, 1], I32)).ap()
    ksq = es.enter_context(nc.sbuf_tensor("a_ksq", [P, SW], BF16)).ap()

    @blk.sync
    def _(s):
        s.dma_start(out=ropeC, in_=d["rope_cos"]).then_inc(c.sems["LD"], 16)
        s.dma_start(out=ropeS, in_=d["rope_sin"]).then_inc(c.sems["LD"], 16)
        s.dma_start(out=masks, in_=d["attn_masks"].rearrange("p (a w) -> p a w", a=4)
                    ).then_inc(c.sems["LD"], 16)
        s.dma_start(out=tok0, in_=d["tok0"]).then_inc(c.sems["LD"], 16)
        c.cnt["LD"] += 64
    ld_stat = c.val("LD")

    @blk.vector
    def _(v):
        v.tensor_copy(Vtok[:, :, 64:65], c.ones_bf[:, 0:32].rearrange(
            "p (a b) -> p a b", b=1))
        ins = v.tensor_copy(Vtok[:, :, 129:130], c.ones_bf[:, 0:32].rearrange(
            "p (a b) -> p a b", b=1))
        c.inc(ins, "VV")
    vv_vones = c.val("VV")

    es2 = ExitStack()
    xlns = es2.enter_context(nc.sbuf_tensor("a_xlns", [P, 2, FC, SW], BF16)).ap()
    lscr = {
        "xb": es2.enter_context(nc.sbuf_tensor("a_lxb", [P, FC, SW], BF16)).ap(),
        "sq": es2.enter_context(nc.sbuf_tensor("a_lsq", [P, 2, SW], BF16)).ap(),
        "ab": es2.enter_context(nc.sbuf_tensor("a_lab", [P, 2 * SW], F32)).ap(),
        "lnt": es2.enter_context(nc.sbuf_tensor("a_lnt", [P, SW], F32)).ap(),
        "rows": es2.enter_context(nc.sbuf_tensor("a_lrows", [1, 4 * SW], F32)).ap(),
    }
    last_mm = [c.val("MM")]
    for sp in range(NS):
        ssl = slice(sp * SW, (sp + 1) * SW)

        @blk.vector
        def _(v, lm=last_mm[0]):
            c.wait(v, "MM", lm)
            if sp >= 2:
                c.wait(v, "XC", 16 * (sp - 1))

        def x_src(fc, ssl=ssl):
            return c.resid[:, fc, ssl]

        def out_bf(fc, sp=sp):
            return xlns[:, sp % 2, fc, :]

        emit_ln(c, x_src, lscr, out_bf)
        last_mm[0] = c.val("MM")
        vv_lnsp = c.val("VV")

        @blk.gpsimd
        def _(g, sp=sp, ssl=ssl, vv_lnsp=vv_lnsp):
            c.wait(g, "VV", vv_lnsp)
            if sp >= 1:
                c.wait(g, "XC", 16 * sp)
            ins = g.dma_start(
                out=d["ccx_in"][:, ssl].rearrange("(fc p) w -> p fc w", p=P),
                in_=xlns[:, sp % 2])
            c.inc(ins, "XC", 16)

    @blk.gpsimd
    def _(g):
        c.wait(g, "XC")
        ins = g.collective_compute(
            "AllGather", ALU.bypass,
            replica_groups=[[0, 1], [2, 3], [4, 5], [6, 7]],
            ins=[d["ccx_in"].opt()], outs=[d["ccx_out"].opt()])
        c.inc(ins, "XC", 1)
    xc_x = c.val("XC")
    es2.close()

    for hp in range(4):
        hp_mm0 = c.val("MM")

        @blk.sync
        def _(g):
            if hp > 0:
                c.wait(g, "MM", hp_mm0)
            for name, tile in (("attn_WqT", wq), ("attn_WkT", wk), ("attn_WvT", wv)):
                ins = g.dma_start(out=tile.rearrange("p a b -> p (a b)"),
                                  in_=d[name][hp])
                c.inc(ins, "WD", 16)
        wd_w = c.val("WD")

        for qs in range(NQ):
            gsl = slice(qs * SW, (qs + 1) * SW)
            par = qs % 2

            rank = qs // (NQ // 2)
            lsl = slice((qs % (NQ // 2)) * SW, (qs % (NQ // 2) + 1) * SW)

            xsem = "LDX" if par == 0 else "LDX2"

            @blk.sync
            def _(g, rank=rank, lsl=lsl, xsem=xsem):
                c.wait(g, "XC", xc_x)
                if qs >= 2:
                    c.wait(g, "MM", c.attn_proj_mm.get((hp, qs - 2), 0))
                elif hp > 0:
                    c.wait(g, "MM", c.attn_proj_mm[(hp - 1, qs + 6)])
                ins = g.dma_start(
                    out=xps[:, par],
                    in_=d["ccx_out"][rank * DM:(rank + 1) * DM, lsl].rearrange(
                        "(fc p) w -> p fc w", p=P))
                c.inc(ins, xsem, 16)
            ld_xp = c.val(xsem)

            mmp0 = c.val("MM")
            vv_prev_strip = c.val("VV")

            @blk.tensor
            def _(t):
                c.wait(t, xsem, ld_xp)
                c.wait(t, "WD", wd_w)
                c.wait(t, "VV", vv_prev_strip)
                if qs == 0 and hp == 0:
                    c.wait(t, "VV", vv_vones)
                for w_t, bank in ((wq, 0), (wk, 1), (wv, 2)):
                    for fc in range(FC):
                        ins = t.matmul(c.pb[bank][:, :], w_t[:, fc, :],
                                       xps[:, par, fc, :],
                                       start=(fc == 0), stop=(fc == FC - 1))
                    c.inc(ins, "MM")
            mm_qkv = c.val("MM")
            c.attn_proj_mm[(hp, qs)] = mm_qkv

            for which, bank, dst in (("q", 0, QT), ("k", 1, KT)):
                vv_r0 = c.val("VV")

                @blk.vector
                def _(v, bank=bank, dst=dst, vv_r0=vv_r0):
                    c.wait(v, "MM", mm_qkv)
                    v.tensor_copy(qr, c.pb[bank][:, :])
                    ins = v.stream_shuffle(qsh, qr, SWAP_MASK)
                    c.inc(ins, "VV")
                pp_r0 = c.val("PP")

                @blk.gpsimd
                def _(g, vv_r0=vv_r0):
                    c.wait(g, "LD", ld_stat)
                    c.wait(g, "VV", vv_r0 + 1)
                    g.tensor_mul(rope_t[:, 0, :], qr, ropeC[:, gsl])
                    ins = g.tensor_mul(rope_t[:, 1, :], qsh, ropeS[:, gsl])
                    c.inc(ins, "PP")

                @blk.vector
                def _(v, dst=dst, pp_r0=pp_r0):
                    c.wait(v, "PP", pp_r0 + 1)
                    ins = v.tensor_add(dst[:, gsl], rope_t[:, 0, :], rope_t[:, 1, :])
                    c.inc(ins, "VV")
            vv_rope = c.val("VV")

            @blk.vector
            def _(v):
                c.wait(v, "MM", mm_qkv)
                ins = v.tensor_copy(vsb, c.pb[2][:, :])
                c.inc(ins, "VV")
            vv_vsb = c.val("VV")

            mmt0 = c.val("MM")

            @blk.tensor
            def _(t):
                c.wait(t, "VV", vv_vsb)
                for j in range(4):
                    if j >= 1:
                        c.wait_fut(t, "VV", vv_vsb + j)
                    ins = t.transpose(c.pbt_bf[:, 0:P], vsb[:, j * P:(j + 1) * P],
                                      c.ident_bf)
                    c.inc(ins, "MM")

            @blk.vector
            def _(v):
                for j in range(4):
                    tt = qs * 4 + j
                    c.wait(v, "MM", mmt0 + j + 1)
                    v.tensor_copy(Vtok[:, tt, 0:64], c.pbt_bf[:, 0:64])
                    ins = v.tensor_copy(Vtok[:, tt, 65:129], c.pbt_bf[:, 64:128])
                    c.inc(ins, "VV")

            ss_b0 = c.val("SS")

            @blk.scalar
            def _(s):
                c.wait(s, "VV", vv_rope)
                ins = s.activation(qsq, QT[:, gsl], AF.Square)
                c.inc(ins, "SS")
                ins = s.activation(ksq, KT[:, gsl], AF.Square)
                c.inc(ins, "SS")

            mmb0 = c.val("MM")
            vvb0 = c.val("VV")

            @blk.tensor
            def _(t):
                for i, sqsrc in ((0, qsq), (1, ksq)):
                    c.wait(t, "SS", ss_b0 + i + 1)
                    for hd in range(2):
                        j = i * 2 + hd
                        if j > 0:
                            c.wait_fut(t, "VV", vvb0 + j)
                        ins = t.matmul(c.pbx[0:1, :],
                                       c.ones_bf[hd * NST:(hd + 1) * NST, 0:1],
                                       sqsrc[hd * NST:(hd + 1) * NST, :],
                                       start=True, stop=True)
                        c.inc(ins, "MM")

            @blk.vector
            def _(v):
                for i in range(2):
                    for hd in range(2):
                        j = i * 2 + hd
                        c.wait(v, "MM", mmb0 + j + 1)
                        ins = v.reduce_max(mx[0:1, i, hd, qs:qs + 1], c.pbx[0:1, :],
                                           mybir.AxisListType.X)
                        c.inc(ins, "VV")

        vv_mx = c.val("VV")
        ss_nb0 = c.val("SS")

        @blk.vector
        def _(v):
            c.wait(v, "VV", vv_mx)
            for hd in range(2):
                v.reduce_max(rowsc[0:1, hd:hd + 1], mx[0:1, 0, hd, :],
                             mybir.AxisListType.X)
                ins = v.reduce_max(rowsc[0:1, 2 + hd:3 + hd], mx[0:1, 1, hd, :],
                                   mybir.AxisListType.X)
            c.inc(ins, "VV")
            c.wait(v, "VV")
            v.tensor_mul(rowsc[0:1, 4:5], rowsc[0:1, 0:1], rowsc[0:1, 2:3])
            ins = v.tensor_mul(rowsc[0:1, 5:6], rowsc[0:1, 1:2], rowsc[0:1, 3:4])
            c.inc(ins, "VV")
        vv_nb = c.val("VV")

        @blk.scalar
        def _(s):
            c.wait(s, "VV", vv_nb)
            ins = s.activation(rowsc[0:1, 6:8], rowsc[0:1, 4:6], AF.Sqrt,
                               bias=c.cvec[0:1, 6:7])
            c.inc(ins, "SS")
        ss_nb = c.val("SS")

        @blk.vector
        def _(v):
            c.wait(v, "SS", ss_nb)
            ins = v.tensor_scalar_mul(rowsc[0:1, 6:8], rowsc[0:1, 6:8],
                                      -0.125 * 1.02)
            c.inc(ins, "VV")
        vv_nb2 = c.val("VV")

        mm_nb0 = c.val("MM")

        @blk.tensor
        def _(t):
            c.wait(t, "VV", vv_nb2)
            for hd in range(2):
                ins = t.matmul(c.pbx[:, hd:hd + 1], c.ones_f32[0:1, 0:P],
                               rowsc[0:1, 6 + hd:7 + hd], start=True, stop=True)
                c.inc(ins, "MM")

        @blk.vector
        def _(v):
            c.wait(v, "MM", mm_nb0 + 2)
            c.wait(v, "SS")
            ins = v.tensor_copy(negB, c.pbx[:, 0:2])
            c.inc(ins, "VV")
        vv_negb = c.val("VV")

        vv_oev_last = [c.val("VV")]
        for qpair in ((0, 1), (2, 3), (4, 5), (6, 7)):
            chains = []
            for ci, qs in enumerate(qpair):
                chains.append({
                    "qs": qs, "nkt": (qs + 1) * 4,
                    "gsl": slice(qs * SW, (qs + 1) * SW),
                    "st": (c.pb[0 + 2 * ci], c.pb[1 + 2 * ci]),
                    "o": ((c.pb[4], c.pb[5]) if ci == 0 else (c.pbx, c.pbt)),
                    "pts": (0 + 2 * ci, 1 + 2 * ci),
                    "ptm": (4 + 2 * ci, 5 + 2 * ci),
                })

            seq = []
            mx_nkt = max(ch["nkt"] for ch in chains)
            for i in range(mx_nkt):
                for ch in chains:
                    if i < ch["nkt"]:
                        seq.append((ch, "S", i))
                for ch in chains:
                    if 0 <= i - 1 < ch["nkt"] - 1:
                        seq.append((ch, "AV", i - 1))
            for ch in chains:
                seq.append((ch, "AV", ch["nkt"] - 1))

            mm0 = c.val("MM")
            ss0 = c.val("SS")
            pp0 = c.val("PP")
            mm_of = {}
            pos = 0
            for ch, kind, kt in seq:
                pos += 2
                mm_of[(ch["qs"], kind, kt)] = mm0 + pos
            ss_of = {}
            scnt = 0
            for ch, kind, kt in seq:
                if kind == "S":
                    scnt += 2
                    ss_of[(ch["qs"], kt)] = ss0 + scnt
            pp_of = {}
            pcnt = 0
            for ch, kind, kt in seq:
                if kind == "S" and (kt * P) >= ch["qs"] * SW:
                    pcnt += 2
                    pp_of[(ch["qs"], kt)] = pp0 + pcnt

            @blk.tensor
            def _(t):
                c.wait(t, "VV", vv_negb)
                c.wait(t, "VV", vv_oev_last[0])
                for ch, kind, kt in seq:
                    qs = ch["qs"]
                    ksl = slice(kt * P, (kt + 1) * P)
                    if kind == "S":
                        if kt >= 1:
                            c.wait_fut(t, "SS", ss_of[(qs, kt - 1)])
                        t.matmul(ch["st"][0][:, :], KT[0:NST, ksl],
                                 QT[0:NST, ch["gsl"]],
                                 start=True, stop=True, tile_position=(0, 0))
                        ins = t.matmul(ch["st"][1][:, :], KT[NST:P, ksl],
                                       QT[NST:P, ch["gsl"]],
                                       start=True, stop=True, tile_position=(64, 0))
                        c.inc(ins, "MM", 2)
                    else:
                        masked = (kt * P) >= qs * SW
                        if masked:
                            c.wait_fut(t, "PP", pp_of[(qs, kt)])
                            sa, sb = ch["ptm"]
                        else:
                            c.wait_fut(t, "SS", ss_of[(qs, kt)])
                            sa, sb = ch["pts"]
                        t.matmul(ch["o"][0][0:65, :], Vtok[:, kt, 0:65],
                                 PT[:, sa, :],
                                 start=(kt == 0), stop=(kt == ch["nkt"] - 1))
                        ins = t.matmul(ch["o"][1][0:65, :], Vtok[:, kt, 65:130],
                                       PT[:, sb, :],
                                       start=(kt == 0), stop=(kt == ch["nkt"] - 1))
                        c.inc(ins, "MM", 2)

            @blk.scalar
            def _(s):
                for ch, kind, kt in seq:
                    if kind != "S":
                        continue
                    qs = ch["qs"]
                    if kt >= 1:
                        c.wait_fut(s, "MM", mm_of[(qs, "AV", kt - 1)])
                    c.wait_fut(s, "MM", mm_of[(qs, "S", kt)])
                    s.activation(PT[:, ch["pts"][0], :], ch["st"][0][:, :],
                                 AF.Exp, bias=negB[:, 0:1], scale=c.cvec[:, 2:3])
                    ins = s.activation(PT[:, ch["pts"][1], :], ch["st"][1][:, :],
                                       AF.Exp, bias=negB[:, 1:2], scale=c.cvec[:, 2:3])
                    c.inc(ins, "SS", 2)

            @blk.gpsimd
            def _(g):
                for ch, kind, kt in seq:
                    qs = ch["qs"]
                    if kind != "S" or (kt * P) < qs * SW:
                        continue
                    mi = (kt * P - qs * SW) // P
                    c.wait_fut(g, "SS", ss_of[(qs, kt)])
                    if kt >= 1:
                        c.wait_fut(g, "MM", mm_of[(qs, "AV", kt - 1)])
                    g.tensor_mul(PT[:, ch["ptm"][0], :], PT[:, ch["pts"][0], :],
                                 masks[:, mi, :])
                    ins = g.tensor_mul(PT[:, ch["ptm"][1], :],
                                       PT[:, ch["pts"][1], :], masks[:, mi, :])
                    c.inc(ins, "PP", 2)

            vv_rc0 = c.val("VV")
            mm_rb0 = c.val("MM")

            @blk.vector
            def _(v):
                if qpair[0] == 0 and hp > 0:
                    c.wait(v, "XC", xc_x + 16 * hp)
                for ci, ch in enumerate(chains):
                    c.wait_fut(v, "MM", mm_of[(ch["qs"], "AV", ch["nkt"] - 1)])
                    v.reciprocal(rrow[:, 0, :], ch["o"][0][NST:NST + 1, :])
                    ins = v.reciprocal(rrow[:, 1, :], ch["o"][1][NST:NST + 1, :])
                    c.inc(ins, "VV")
                    for hd in range(2):
                        c.wait_fut(v, "MM", mm_rb0 + ci * 2 + hd + 1)
                        ins = v.tensor_copy(rb, c.pb[0][0:NST, :])
                        c.inc(ins, "VV")
                        ins = v.tensor_tensor(
                            OT[hd * NST:(hd + 1) * NST, ch["gsl"]],
                            ch["o"][hd][0:NST, :], rb, ALU.mult)
                        c.inc(ins, "VV")

            @blk.tensor
            def _(t):
                free_at = [None, vv_rc0 + 2, vv_rc0 + 4, vv_rc0 + 7]
                for ci in range(2):
                    for hd in range(2):
                        j = ci * 2 + hd
                        c.wait_fut(t, "VV", vv_rc0 + ci * 5 + 1)
                        if j >= 1:
                            c.wait_fut(t, "VV", free_at[j])
                        ins = t.matmul(c.pb[0][0:NST, :], c.ones_f32[0:1, 0:NST],
                                       rrow[0:1, hd, :], start=True, stop=True)
                        c.inc(ins, "MM")
            assert c.val("MM") == mm_rb0 + 4
            vv_oev_last[0] = c.val("VV")

        vv_pair = c.val("VV")

        @blk.gpsimd
        def _(g):
            c.wait(g, "VV", vv_pair)
            ins = g.dma_start(out=d["cco_in"][hp * P:(hp + 1) * P, :], in_=OT)
            c.inc(ins, "XC", 16)

    @blk.gpsimd
    def _(g):
        c.wait(g, "XC")
        ins = g.collective_compute(
            "AllGather", ALU.bypass,
            replica_groups=[[0, 1], [2, 3], [4, 5], [6, 7]],
            ins=[d["cco_in"].opt()], outs=[d["cco_out"].opt()])
        c.inc(ins, "XC", 1)
    xc_o = c.val("XC")

    wo = es.enter_context(nc.sbuf_tensor("a_wo", [P, FC, DM], BF16)).ap()

    mm_gate_wo = c.val("MM")

    @blk.sync
    def _(g):
        c.wait(g, "MM", mm_gate_wo)
        ins = g.dma_start(out=wo.rearrange("p a b -> p (a b)"), in_=d["attn_WoT"])
        c.inc(ins, "WD", 16)
    wd_wo = c.val("WD")

    for sp in range(NS):
        ssl = slice(sp * SW, (sp + 1) * SW)
        par = sp % 2

        xsem = "LDX" if par == 0 else "LDX2"

        @blk.sync
        def _(g, xsem=xsem):
            c.wait(g, "XC", xc_o)
            c.wait(g, "LD", ld_stat)
            if sp >= 2:
                c.wait(g, "MM", c.attn_oproj_mm.get(sp - 2, 0))
            r = g.value_load(tok0)
            src = d["cco_out"][:, sp * SW:(sp + 1) * SW].rearrange(
                "(fc p) w -> p fc w", p=P)
            ins = g.dma_start(out=xat[:, par], in_=dynoff(src, r))
            c.inc(ins, xsem, 16)
        ld_at = c.val(xsem)

        mmo0 = c.val("MM")
        vv_oe0 = c.val("VV")

        ss_cur = c.val("SS")

        @blk.tensor
        def _(t):
            c.wait(t, xsem, ld_at)
            c.wait(t, "WD", wd_wo)
            c.wait(t, "SS", ss_cur)
            for oc in range(FC):
                bank = c.pb[0 + (oc % 2)]
                if oc >= 2:
                    c.wait_fut(t, "VV", vv_oe0 + oc - 1)
                for fc in range(FC):
                    ins = t.matmul(bank[:, :], wo[:, fc, oc * P:(oc + 1) * P],
                                   xat[:, par, fc, :],
                                   start=(fc == 0), stop=(fc == FC - 1))
                c.inc(ins, "MM")
        c.attn_oproj_mm[sp] = c.val("MM")

        @blk.vector
        def _(v):
            for oc in range(FC):
                bank = c.pb[0 + (oc % 2)]
                c.wait(v, "MM", mmo0 + oc + 1)
                ins = v.tensor_tensor(c.resid[:, oc, ssl], bank[:, :],
                                      c.resid[:, oc, ssl], ALU.add)
                c.inc(ins, "VV")


PHASES_DEFAULT = "ssm,moe0,attn,moe1"


def build_program(phases):
    nc = bass.Bass("TRN2", target_bir_lowering=False)
    d = {}

    def din(name, shape, dtype=F32):
        d[name] = nc.dram_tensor(name, shape, dtype, kind="ExternalInput").ap()

    din("xT_own", [DM, T])
    din("xT_prev", [P, FC, T], BF16)
    din("ones_f32", [P, P])
    din("cvec", [P, 8])
    din("ones_bf", [P, P], BF16)
    din("ident_f32", [P, P])
    din("ident_bf", [P, P], BF16)
    d["outT"] = nc.dram_tensor("outT", [DM, T], F32, kind="ExternalOutput").ap()
    if os.environ.get("KDBG"):
        d["dbg"] = nc.dram_tensor("dbg", [64, T], F32, kind="ExternalOutput").ap()
    if os.environ.get("KDBG2") or os.environ.get("KDBG3"):
        d["dbg2"] = nc.dram_tensor("dbg2", [P, 4 * T], F32, kind="ExternalOutput").ap()
    if "ssm" in phases:
        din("ssm_logLam", [NST, 1])
        din("ssm_LamIm", [NST, 1])
        din("ssm_logdt", [1, 1])
        din("ssm_BrT", [P, FC * NST], BF16)
        din("ssm_BiT", [P, FC * NST], BF16)
        din("ssm_CC", [P, DM], BF16)
        din("ssm_WoutT", [P, FC * DM], BF16)
        din("ssm_D", [P, FC])
        din("ssm_bout", [P, FC])
    if "attn" in phases:
        din("rope_cos", [P, FT], BF16)
        din("rope_sin", [P, FT], BF16)
        din("attn_masks", [P, 4 * SW], BF16)
        din("tok0", [1, 1], I32)
        din("attn_WqT", [4, P, FC * P], BF16)
        din("attn_WkT", [4, P, FC * P], BF16)
        din("attn_WvT", [4, P, FC * P], BF16)
        din("attn_WoT", [P, FC * DM], BF16)
        d["ccx_in"] = nc.dram_tensor("ccx_in", [DM, T], BF16, kind="Internal").ap()
        d["ccx_out"] = nc.dram_tensor("ccx_out", [2 * DM, T], BF16, kind="Internal").ap()
        d["cco_in"] = nc.dram_tensor("cco_in", [4 * P, FT], BF16, kind="Internal").ap()
        d["cco_out"] = nc.dram_tensor("cco_out", [DM, FT], BF16, kind="Internal").ap()
    for pfx in ("moe0", "moe1"):
        if pfx in phases:
            din(pfx + "_gate", [DM, NE])
            din(pfx + "_w1", [NE, HID // 512, P, FC * 512], BF16)
            din(pfx + "_w3", [NE, HID // 512, P, FC * 512], BF16)
            din(pfx + "_w2", [NE, HID // 512, P, 4 * DM], BF16)

    es = ExitStack()
    sems = {name: es.enter_context(nc.semaphore(name)) for name in SEM_NAMES}
    resid = es.enter_context(nc.sbuf_tensor("resid", [P, FC, T], F32))
    ones_f32 = es.enter_context(nc.sbuf_tensor("ones_f32s", [P, P], F32))
    cvec = es.enter_context(nc.sbuf_tensor("cvecs", [P, 8], F32))
    ident_f32 = es.enter_context(nc.sbuf_tensor("ident_f32s", [P, P], F32))
    ident_bf_s = es.enter_context(nc.sbuf_tensor("ident_bfs", [P, P], BF16))
    ones_bf = es.enter_context(nc.sbuf_tensor("ones_bfs", [P, P], BF16))
    pb = [es.enter_context(nc.psum_tensor(f"pb{i}", [P, SW], F32)) for i in range(6)]
    pbt = es.enter_context(nc.psum_tensor("pbt", [P, SW], F32))
    pbx = es.enter_context(nc.psum_tensor("pbx", [P, SW], F32))

    block = es.enter_context(nc.Block())
    c = Ctx(nc, block, sems)
    c.d = d
    c.resid = resid.ap()
    c.ones_f32 = ones_f32.ap()
    c.cvec = cvec.ap()
    c.ident_f32 = ident_f32.ap()
    c.ident_bf = ident_bf_s.ap()
    c.ones_bf = ones_bf.ap()
    c.pb = [b.ap() for b in pb]
    c.pbt = pbt.ap()
    c.pbt_bf = pbt.ap().bitcast(BF16)
    c.pbx = pbx.ap()

    @block.sync
    def _(s):
        ins = s.dma_start(out=c.resid,
                          in_=d["xT_own"].rearrange("(fc p) t -> p fc t", p=P))
        c.inc(ins, "LD", 16)
        ins = s.dma_start(out=c.ones_f32, in_=d["ones_f32"])
        c.inc(ins, "LD", 16)
        ins = s.dma_start(out=c.ones_bf, in_=d["ones_bf"])
        c.inc(ins, "LD", 16)
        ins = s.dma_start(out=c.cvec, in_=d["cvec"])
        c.inc(ins, "LD", 16)
        ins = s.dma_start(out=c.ident_f32, in_=d["ident_f32"])
        c.inc(ins, "LD", 16)
        ins = s.dma_start(out=c.ident_bf, in_=d["ident_bf"])
        c.inc(ins, "LD", 16)
    ld_init = c.val("LD")

    for name in ("tensor", "vector", "scalar", "gpsimd"):
        def body(eng, v=ld_init):
            eng.wait_ge(sems["LD"], v)
        getattr(block, name)(body)

    if "ssm" in phases:
        phase_es = ExitStack()
        emit_ssm(c, phase_es)
        emit_barrier(c)
        phase_es.close()
    if "moe0" in phases:
        phase_es = ExitStack()
        emit_moe(c, phase_es, "moe0", 1.0)
        emit_barrier(c)
        phase_es.close()
    if "attn" in phases:
        phase_es = ExitStack()
        emit_attn(c, phase_es)
        emit_barrier(c)
        phase_es.close()
    if "moe1" in phases:
        phase_es = ExitStack()
        emit_moe(c, phase_es, "moe1", DEPTH_SCALE)
        emit_barrier(c)
        phase_es.close()

    @block.gpsimd
    def _(g):
        ins = g.dma_start(out=d["outT"].rearrange("(fc p) t -> p fc t", p=P),
                          in_=c.resid)
        c.inc(ins, "ST", 16)
        c.wait(g, "ST")

    es.close()
    return nc



def w_up(w):
    w = np.asarray(w)
    a = w.reshape(NE, FC, P, HID // 512, 512).transpose(0, 3, 2, 1, 4)
    return np.ascontiguousarray(a.reshape(NE, HID // 512, P, FC * 512)).astype(bfd)


def w_dn(w):
    w = np.asarray(w)
    a = w.reshape(NE, HID // 512, 4, P, DM).transpose(0, 1, 3, 2, 4)
    return np.ascontiguousarray(a.reshape(NE, HID // 512, P, 4 * DM)).astype(bfd)


def w_qkv(w, h):
    a = np.asarray(w).T[:, h * 512:(h + 1) * 512]
    a = a.reshape(FC, P, 4, P).transpose(2, 1, 0, 3)
    return np.ascontiguousarray(a.reshape(4, P, FC * P)).astype(bfd)


def prep_inputs(inputs, phases):
    x = np.asarray(inputs["L0_attn_x"])
    ones_f32 = np.ones((P, P), dtype=np.float32)
    ones_bf = np.ones((P, P), dtype=bfd)
    ident_f32 = np.eye(P, dtype=np.float32)
    ident_bf = np.eye(P).astype(bfd)
    invfreq = 1.0 / (10000.0 ** (np.arange(0, 64, 2) / 64.0))
    dd = np.arange(64)
    ang = np.arange(FT)[None, :] * invfreq[dd % 32][:, None]
    cos_t = np.cos(ang)
    sin_t = np.sin(ang)
    sin_sgn = np.where((dd % 2 == 0)[:, None], -sin_t, sin_t)
    rope_cos = np.ascontiguousarray(np.tile(cos_t, (2, 1))).astype(bfd)
    rope_sin = np.ascontiguousarray(np.tile(sin_sgn, (2, 1))).astype(bfd)
    attn_masks_f = np.zeros((P, 4, SW), dtype=np.float32)
    for mi in range(4):
        attn_masks_f[:, mi, :] = (np.arange(P)[:, None] + mi * P
                                  <= np.arange(SW)[None, :])
    attn_masks = np.ascontiguousarray(attn_masks_f.reshape(P, 4 * SW)).astype(bfd)
    cv = np.zeros((P, 8), dtype=np.float32)
    cv[:, 0] = np.pi / 2
    cv[:, 1] = 1e-5
    cv[:, 2] = 0.125
    cv[:, 3] = -1.0
    cv[:, 4] = -0.5
    cv[:, 5] = 0.5
    maps = []
    for core in range(8):
        b, h = core // 2, core % 2
        m = {
            "xT_own": np.ascontiguousarray(x[b, h * T:(h + 1) * T, :].T),
            "xT_prev": (np.zeros((P, FC, T), bfd) if h == 0
                        else np.ascontiguousarray(
                            x[b, 0:T, :].T.reshape(FC, P, T).transpose(1, 0, 2)
                        ).astype(bfd)),
            "ones_f32": ones_f32,
            "ones_bf": ones_bf,
            "cvec": cv,
            "ident_f32": ident_f32,
            "ident_bf": ident_bf,
        }
        if "ssm" in phases:
            m["ssm_logLam"] = np.ascontiguousarray(
                np.asarray(inputs["L0_attn_log_Lam_real"]).reshape(NST, 1))
            m["ssm_LamIm"] = np.ascontiguousarray(
                np.asarray(inputs["L0_attn_Lam_imag"]).reshape(NST, 1))
            m["ssm_logdt"] = np.ascontiguousarray(
                np.asarray(inputs["L0_attn_log_dt"]).reshape(1, 1))
            m["ssm_BrT"] = np.ascontiguousarray(
                np.asarray(inputs["L0_attn_B_real"]).T.reshape(FC, P, NST)
                .transpose(1, 0, 2).reshape(P, FC * NST)).astype(bfd)
            m["ssm_BiT"] = np.ascontiguousarray(
                np.asarray(inputs["L0_attn_B_imag"]).T.reshape(FC, P, NST)
                .transpose(1, 0, 2).reshape(P, FC * NST)).astype(bfd)
            m["ssm_CC"] = np.ascontiguousarray(np.concatenate(
                [np.asarray(inputs["L0_attn_C_real"]).T,
                 np.asarray(inputs["L0_attn_C_imag"]).T], axis=0)).astype(bfd)
            m["ssm_WoutT"] = np.ascontiguousarray(
                np.asarray(inputs["L0_attn_W_out"]).T.reshape(FC, P, DM)
                .transpose(1, 0, 2).reshape(P, FC * DM)).astype(bfd)
            m["ssm_D"] = np.ascontiguousarray(
                np.asarray(inputs["L0_attn_D"]).reshape(FC, P).T)
            m["ssm_bout"] = np.ascontiguousarray(
                np.asarray(inputs["L0_attn_b_out"]).reshape(FC, P).T)
        if "moe0" in phases:
            m["moe0_gate"] = np.ascontiguousarray(np.asarray(inputs["L0_ffn_gate"]))
            m["moe0_w1"] = w_up(inputs["L0_ffn_experts_w1"])
            m["moe0_w3"] = w_up(inputs["L0_ffn_experts_w3"])
            m["moe0_w2"] = w_dn(inputs["L0_ffn_experts_w2"])
        if "attn" in phases:
            m["attn_WqT"] = w_qkv(inputs["L1_attn_Wq"], h)
            m["attn_WkT"] = w_qkv(inputs["L1_attn_Wk"], h)
            m["attn_WvT"] = w_qkv(inputs["L1_attn_Wv"], h)
            m["attn_WoT"] = np.ascontiguousarray(
                np.asarray(inputs["L1_attn_Wo"]).T.reshape(FC, P, DM)
                .transpose(1, 0, 2).reshape(P, FC * DM)).astype(bfd)
            m["rope_cos"] = rope_cos
            m["rope_sin"] = rope_sin
            m["attn_masks"] = attn_masks
            m["tok0"] = np.array([[h * T]], dtype=np.int32)
        if "moe1" in phases:
            m["moe1_gate"] = np.ascontiguousarray(np.asarray(inputs["L1_ffn_Wr"]))
            m["moe1_w1"] = w_up(inputs["L1_ffn_Wg"])
            m["moe1_w3"] = w_up(inputs["L1_ffn_Wu"])
            m["moe1_w2"] = w_dn(inputs["L1_ffn_Wd"])
        maps.append(m)
    return maps


def kernel(**inputs):
    phases = os.environ.get("KPHASES", PHASES_DEFAULT).split(",")
    maps = prep_inputs(inputs, phases)
    nc = build_program(phases)
    res = run_bass_kernel_spmd(nc, maps, core_ids=list(range(8)))
    out = np.empty((4, FT, DM), dtype=np.float32)
    for core in range(8):
        b, h = core // 2, core % 2
        out[b, h * T:(h + 1) * T, :] = res.results[core]["outT"].T
    return out


# version 05
# speedup vs baseline: 1.0623x; 1.0623x over previous
import dataclasses
import os
from contextlib import ExitStack

import numpy as np
import ml_dtypes

import concourse.bass as bass
import concourse.mybir as mybir
from concourse.bass_utils import run_bass_kernel_spmd

F32 = mybir.dt.float32
BF16 = mybir.dt.bfloat16
I32 = mybir.dt.int32
ALU = mybir.AluOpType
AF = mybir.ActivationFunctionType

P = 128
T = 2048
FT = 4096
SW = 512
NS = 4
FC = 8
DM = 1024
NST = 64
HID = 4096
NE = 8
DEPTH_SCALE = 1.0 / np.sqrt(12.0)
INV_2PI = float(1.0 / (2.0 * np.pi))
TWO_PI = float(2.0 * np.pi)

bfd = np.dtype(ml_dtypes.bfloat16)

SEM_NAMES = ("LD", "WD", "WD2", "ROWD", "LDX", "LDX2", "MM", "VV", "SS", "PP", "ST", "XC", "BAR")
DATA_SEMS = ("LD", "WD", "WD2", "ROWD", "LDX", "LDX2", "MM", "VV", "SS", "PP", "ST", "XC")


class Ctx:
    def __init__(self, nc, block, sems):
        self.nc = nc
        self.block = block
        self.sems = sems
        self.cnt = {k: 0 for k in sems}

    def inc(self, ins, name, n=1):
        self.cnt[name] += n
        ins.then_inc(self.sems[name], n)
        return self.cnt[name]

    def val(self, name):
        return self.cnt[name]

    def wait(self, eng, name, value=None):
        v = self.cnt[name] if value is None else value
        assert v <= self.cnt[name], f"wait on future {name}: {v} > {self.cnt[name]}"
        if v > 0:
            eng.wait_ge(self.sems[name], v)

    def wait_fut(self, eng, name, v):
        if v > 0:
            eng.wait_ge(self.sems[name], v)


def dynoff(ap, extra):
    return dataclasses.replace(ap, offset=ap.offset + extra)


def emit_barrier(c):
    snap = {k: c.cnt[k] for k in DATA_SEMS}
    c.cnt["BAR"] += 1
    tgt = c.cnt["BAR"]

    @c.block.gpsimd
    def _(g):
        for k, v in snap.items():
            if v:
                g.wait_ge(c.sems[k], v)
        g.sem_inc(c.sems["BAR"], 1)

    for name in ("sync", "tensor", "vector", "scalar"):
        def body(eng, tgt=tgt):
            eng.wait_ge(c.sems["BAR"], tgt)
        getattr(c.block, name)(body)



def emit_ln(c, x_src, scr, out_bf, retain=None, src_is_bf=False, rows_tok=None):
    pb0, pb1, pb2, pb3 = c.pb[0], c.pb[1], c.pb[2], c.pb[3]
    xb, sq, ab = scr["xb"], scr["sq"], scr["ab"]
    rows = scr["rows"]

    if src_is_bf:
        def xb_src(fc):
            return x_src(fc)
        vv_cast = c.val("VV")
    else:
        @c.block.vector
        def _(v):
            for fc in range(FC):
                ins = v.tensor_copy(xb[:, fc, :], x_src(fc))
            c.inc(ins, "VV")
        vv_cast = c.val("VV")

        def xb_src(fc):
            return xb[:, fc, :]

    ss_sq0 = c.val("SS")
    mm_sq0 = c.val("MM")

    @c.block.scalar
    def _(s):
        c.wait(s, "VV", vv_cast)
        for fc in range(FC):
            if fc >= 2:
                c.wait_fut(s, "MM", mm_sq0 + fc - 1)
            ins = s.activation(sq[:, fc % 2, :], xb_src(fc), AF.Square)
            c.inc(ins, "SS")

    @c.block.tensor
    def _(t):
        c.wait(t, "VV", vv_cast)
        for fc in range(FC):
            t.matmul(pb0[0:1, :], c.ones_bf[0:P, 0:1], xb_src(fc),
                     start=(fc == 0), stop=(fc == FC - 1))
        for fc in range(FC):
            c.wait(t, "SS", ss_sq0 + fc + 1)
            ins = t.matmul(pb1[0:1, :], c.ones_bf[0:P, 0:1], sq[:, fc % 2, :],
                           start=(fc == 0), stop=(fc == FC - 1))
            c.inc(ins, "MM")
    mm_sums = c.val("MM")
    assert mm_sums == mm_sq0 + FC

    mean = rows[0:1, 0:SW]
    msq = rows[0:1, SW:2 * SW]
    var = rows[0:1, 2 * SW:3 * SW]
    sd = rows[0:1, SW:2 * SW]
    a_row = rows[0:1, 2 * SW:3 * SW]
    b_row = rows[0:1, 3 * SW:4 * SW]

    @c.block.vector
    def _(v):
        c.wait(v, "MM", mm_sums)
        v.tensor_scalar_mul(mean, pb0[0:1, :], 1.0 / DM)
        v.tensor_mul(msq, mean, mean)
        ins = v.scalar_tensor_tensor(var, pb1[0:1, :], 1.0 / DM, msq,
                                     ALU.mult, ALU.subtract)
        c.inc(ins, "VV")
    vv_var = c.val("VV")

    @c.block.scalar
    def _(s):
        c.wait(s, "VV", vv_var)
        ins = s.activation(sd, var, AF.Sqrt, bias=c.cvec[0:1, 1:2])
        c.inc(ins, "SS")
    ss_sd = c.val("SS")

    @c.block.vector
    def _(v):
        c.wait(v, "SS", ss_sd)
        ins = v.reciprocal(a_row, sd)
        c.inc(ins, "VV")
        c.wait(v, "VV")
        ins = v.scalar_tensor_tensor(b_row, mean, -1.0, a_row, ALU.mult, ALU.mult)
        c.inc(ins, "VV")
    vv_b = c.val("VV")

    @c.block.tensor
    def _(t):
        c.wait(t, "VV", vv_b)
        t.matmul(pb2[:, :], c.ones_f32[0:1, 0:P], a_row, start=True, stop=True)
        ins = t.matmul(pb3[:, :], c.ones_f32[0:1, 0:P], b_row, start=True, stop=True)
        c.inc(ins, "MM")
    mm_ab = c.val("MM")

    if rows_tok is not None:
        atok, btok, sp0 = rows_tok

        @c.block.tensor
        def _(t):
            for j in range(4):
                t.transpose(c.pb[5][:, j:j + 1], rows[0:1, 2 * SW + j * P:2 * SW + (j + 1) * P],
                            c.ident_f32[0:1, 0:1])
            for j in range(4):
                ins = t.transpose(c.pb[5][:, 4 + j:5 + j],
                                  rows[0:1, 3 * SW + j * P:3 * SW + (j + 1) * P],
                                  c.ident_f32[0:1, 0:1])
            c.inc(ins, "MM")
        mm_rt = c.val("MM")

        @c.block.vector
        def _(v):
            c.wait(v, "MM", mm_rt)
            v.tensor_copy(atok[:, sp0 * 4:(sp0 + 1) * 4], c.pb[5][:, 0:4])
            ins = v.tensor_copy(btok[:, sp0 * 4:(sp0 + 1) * 4], c.pb[5][:, 4:8])
            c.inc(ins, "VV")

    a_bc = ab[:, 0:SW]
    b_bc = ab[:, SW:2 * SW]

    @c.block.vector
    def _(v):
        c.wait(v, "MM", mm_ab)
        v.tensor_copy(a_bc, pb2[:, :])
        ins = v.tensor_copy(b_bc, pb3[:, :])
        c.inc(ins, "VV")
        if retain is not None:
            v.tensor_copy(retain[0], a_bc)
            ins = v.tensor_copy(retain[1], b_bc)
            c.inc(ins, "VV")
        for fc in range(FC):
            v.tensor_mul(scr["lnt"], x_src(fc), a_bc)
            ins = v.tensor_add(out_bf(fc), scr["lnt"], b_bc)
        c.inc(ins, "VV")



def emit_ssm(c, es):
    nc, blk = c.nc, c.block
    d = c.d

    logLam = es.enter_context(nc.sbuf_tensor("s_logLam", [NST, 1], F32)).ap()
    LamIm = es.enter_context(nc.sbuf_tensor("s_LamIm", [NST, 1], F32)).ap()
    logdt = es.enter_context(nc.sbuf_tensor("s_logdt", [1, 1], F32)).ap()
    dts = es.enter_context(nc.sbuf_tensor("s_dts", [1, 1], F32)).ap()
    dtc = es.enter_context(nc.sbuf_tensor("s_dtc", [P, 1], F32)).ap()
    mcol = es.enter_context(nc.sbuf_tensor("s_mcol", [P, 1], F32)).ap()
    th = es.enter_context(nc.sbuf_tensor("s_th", [NST, 1], F32)).ap()
    thm = es.enter_context(nc.sbuf_tensor("s_thm", [NST, 1], F32)).ap()
    ki = es.enter_context(nc.sbuf_tensor("s_ki", [NST, 1], I32)).ap()
    kf = es.enter_context(nc.sbuf_tensor("s_kf", [NST, 1], F32)).ap()
    hs = es.enter_context(nc.sbuf_tensor("s_hs", [NST, 1], F32)).ap()
    hc = es.enter_context(nc.sbuf_tensor("s_hc", [NST, 1], F32)).ap()
    sc1 = es.enter_context(nc.sbuf_tensor("s_sc1", [NST, 1], F32)).ap()
    cc1 = es.enter_context(nc.sbuf_tensor("s_cc1", [NST, 1], F32)).ap()
    piv = es.enter_context(nc.sbuf_tensor("s_piv", [NST, 4], F32)).ap()
    lr = es.enter_context(nc.sbuf_tensor("s_lr", [NST, 1], F32)).ap()

    BrT = es.enter_context(nc.sbuf_tensor("s_BrT", [P, FC, NST], BF16)).ap()
    BiT = es.enter_context(nc.sbuf_tensor("s_BiT", [P, FC, NST], BF16)).ap()
    CCw = es.enter_context(nc.sbuf_tensor("s_CC", [P, DM], BF16)).ap()
    WoT = es.enter_context(nc.sbuf_tensor("s_WoT", [P, FC, DM], BF16)).ap()
    Dcol = es.enter_context(nc.sbuf_tensor("s_Dcol", [P, FC], F32)).ap()
    bout = es.enter_context(nc.sbuf_tensor("s_bout", [P, FC], F32)).ap()

    TBL = es.enter_context(nc.sbuf_tensor("s_TBL", [P, FT], F32)).ap()
    TBL2 = es.enter_context(nc.sbuf_tensor("s_TBL2", [P, FT], F32)).ap()
    ct = TBL[0:NST, :]
    st = TBL2[0:NST, :]
    ct_hi = TBL2[NST:P, :]
    st_hi = TBL[NST:P, :]
    U = es.enter_context(nc.sbuf_tensor("s_U", [P, FT], F32)).ap()
    W = es.enter_context(nc.sbuf_tensor("s_W", [P, T], F32)).ap()
    scrA = es.enter_context(nc.sbuf_tensor("s_scrA", [P, T // 2], F32)).ap()
    scrB = es.enter_context(nc.sbuf_tensor("s_scrB", [P, T // 2], F32)).ap()
    Sst = es.enter_context(nc.sbuf_tensor("s_Sst", [P, T], BF16)).ap()
    ybf = es.enter_context(nc.sbuf_tensor("s_ybf", [P, FC, SW], BF16)).ap()
    afull = es.enter_context(nc.sbuf_tensor("s_afull", [P, T], F32)).ap()
    bfull = es.enter_context(nc.sbuf_tensor("s_bfull", [P, T], F32)).ap()
    xprev = es.enter_context(nc.sbuf_tensor("s_xprev", [P, FC, SW], BF16)).ap()
    xlnb = xprev
    lscr = {
        "xb": xprev,
        "sq": es.enter_context(nc.sbuf_tensor("s_lsq", [P, 2, SW], BF16)).ap(),
        "ab": es.enter_context(nc.sbuf_tensor("s_lab", [P, 2 * SW], F32)).ap(),
        "lnt": es.enter_context(nc.sbuf_tensor("s_lnt", [P, SW], F32)).ap(),
        "rows": es.enter_context(nc.sbuf_tensor("s_lrows", [1, 4 * SW], F32)).ap(),
    }
    pscr = es.enter_context(nc.sbuf_tensor("s_pscr", [P, 2, SW], F32)).ap()

    @blk.sync
    def _(s):
        c.wait(s, "LD")
        for name, tile in (("ssm_logLam", logLam), ("ssm_LamIm", LamIm),
                           ("ssm_logdt", logdt), ("ssm_D", Dcol),
                           ("ssm_bout", bout)):
            ins = s.dma_start(out=tile, in_=d[name])
            c.inc(ins, "LD", 16)
    ld_small = c.val("LD")

    @blk.sync
    def _(g):
        ins = g.dma_start(out=BrT.rearrange("p a b -> p (a b)"), in_=d["ssm_BrT"])
        c.inc(ins, "WD", 16)
        ins = g.dma_start(out=BiT.rearrange("p a b -> p (a b)"), in_=d["ssm_BiT"])
        c.inc(ins, "WD", 16)
        ins = g.dma_start(out=CCw, in_=d["ssm_CC"])
        c.inc(ins, "WD", 16)
        ins = g.dma_start(out=WoT.rearrange("p a b -> p (a b)"), in_=d["ssm_WoutT"])
        c.inc(ins, "WD", 16)
    wd_wts = c.val("WD")

    @blk.scalar
    def _(s):
        c.wait(s, "LD", ld_small)
        s.activation(dts, logdt, AF.Exp)
        ins = s.activation(lr, logLam, AF.Exp)
        c.inc(ins, "SS")
    ss_dt = c.val("SS")

    @blk.tensor
    def _(t):
        c.wait(t, "SS", ss_dt)
        ins = t.matmul(c.pb[4][:, 0:1], c.ones_f32[0:1, 0:P], dts,
                       start=True, stop=True)
        c.inc(ins, "MM")
    mm_dt = c.val("MM")

    @blk.vector
    def _(v):
        c.wait(v, "MM", mm_dt)
        ins = v.tensor_copy(dtc, c.pb[4][:, 0:1])
        c.inc(ins, "VV")
        c.wait(v, "VV")
        v.tensor_scalar(th, LamIm, dtc[0:NST, :], None, ALU.mult)
        ins = v.tensor_scalar(kf, lr, dtc[0:NST, :], None, ALU.mult)
        c.inc(ins, "VV")
    vv_th = c.val("VV")

    @blk.scalar
    def _(s):
        c.wait(s, "VV", vv_th)
        ins = s.activation(mcol[0:NST, :], kf, AF.Exp, scale=c.cvec[0:NST, 3:4])
        c.inc(ins, "SS")
    ss_m = c.val("SS")

    @blk.vector
    def _(v):
        c.wait(v, "SS", ss_m)
        v.tensor_copy(mcol[NST:P, :], mcol[0:NST, :])
        ins = v.tensor_scalar_mul(kf, th, INV_2PI)
        c.inc(ins, "VV")
        c.wait(v, "VV")
        ins = v.tensor_copy(ki, kf)
        c.inc(ins, "VV")
        c.wait(v, "VV")
        ins = v.tensor_copy(kf, ki)
        c.inc(ins, "VV")
        c.wait(v, "VV")
        ins = v.scalar_tensor_tensor(thm, kf, -TWO_PI, th, ALU.mult, ALU.add)
        c.inc(ins, "VV")
    vv_thm = c.val("VV")

    @blk.scalar
    def _(s):
        c.wait(s, "VV", vv_thm)
        s.activation(hs, thm, AF.Sin, scale=c.cvec[0:NST, 5:6])
        ins = s.activation(hc, thm, AF.Sin, scale=c.cvec[0:NST, 4:5], bias=c.cvec[0:NST, 0:1])
        c.inc(ins, "SS")
    ss_half = c.val("SS")

    @blk.vector
    def _(v):
        c.wait(v, "SS", ss_half)
        ins = v.scalar_tensor_tensor(sc1, hs, 2.0, hc, ALU.mult, ALU.mult)
        c.inc(ins, "VV")
        v.tensor_mul(cc1, hs, hs)
        c.wait(v, "VV")
        ins = v.scalar_tensor_tensor(cc1, cc1, -2.0, c.ones_f32[0:NST, 0:1],
                                     ALU.mult, ALU.add)
        c.inc(ins, "VV")
    vv_cs = c.val("VV")

    @blk.vector
    def _(v):
        c.wait(v, "VV", vv_cs)
        v.memset(st[:, 0:1], 0.0)
        v.tensor_copy(ct[:, 0:1], c.ones_f32[0:NST, 0:1])
        v.tensor_copy(ct[:, 1:2], cc1)
        ins = v.tensor_copy(st[:, 1:2], sc1)
        c.inc(ins, "VV")
        have = 2
        while have < FT:
            ext = min(have, FT - have, T // 2)
            J = have
            cJ, sJ, nsJ, tmp = piv[:, 0:1], piv[:, 1:2], piv[:, 2:3], piv[:, 3:4]
            c.wait(v, "VV")
            v.tensor_scalar(tmp, ct[:, J - 1:J], cc1, None, ALU.mult)
            ins = v.scalar_tensor_tensor(cJ, st[:, J - 1:J], sc1, tmp,
                                         ALU.mult, ALU.subtract)
            c.inc(ins, "VV")
            c.wait(v, "VV")
            ins = v.tensor_scalar_mul(cJ, cJ, -1.0)
            c.inc(ins, "VV")
            v.tensor_scalar(tmp, st[:, J - 1:J], cc1, None, ALU.mult)
            c.wait(v, "VV")
            ins = v.scalar_tensor_tensor(sJ, ct[:, J - 1:J], sc1, tmp,
                                         ALU.mult, ALU.add)
            c.inc(ins, "VV")
            c.wait(v, "VV")
            ins = v.tensor_scalar_mul(nsJ, sJ, -1.0)
            c.inc(ins, "VV")
            c.wait(v, "VV")
            v.tensor_scalar(scrA[0:NST, 0:ext], ct[:, 0:ext], cJ, None, ALU.mult)
            ins = v.scalar_tensor_tensor(ct[:, J:J + ext], st[:, 0:ext], nsJ,
                                         scrA[0:NST, 0:ext], ALU.mult, ALU.add)
            c.inc(ins, "VV")
            v.tensor_scalar(scrA[0:NST, 0:ext], st[:, 0:ext], cJ, None, ALU.mult)
            c.wait(v, "VV")
            ins = v.scalar_tensor_tensor(st[:, J:J + ext], ct[:, 0:ext], sJ,
                                         scrA[0:NST, 0:ext], ALU.mult, ALU.add)
            c.inc(ins, "VV")
            have += ext
        c.wait(v, "VV")
        v.tensor_copy(ct_hi, ct)
        ins = v.tensor_copy(st_hi, st)
        c.inc(ins, "VV")

    last_bu_mm = [0]

    def bu_strip(half, sp):
        pos = half * T + sp * SW
        if half == 0:
            @blk.sync
            def _(g):
                c.wait(g, "MM", last_bu_mm[0])
                if sp == 0:
                    c.wait(g, "LD", ld_small)
                ins = g.dma_start(
                    out=xprev, in_=d["xT_prev"][:, :, sp * SW:(sp + 1) * SW])
                c.inc(ins, "LD", 16)
            ld_strip = c.val("LD")

            @blk.vector
            def _(v):
                c.wait(v, "LD", ld_strip)
                c.wait(v, "MM", last_bu_mm[0])

            @blk.scalar
            def _(s):
                c.wait(s, "LD", ld_strip)

            @blk.tensor
            def _(t):
                c.wait(t, "LD", ld_strip)

            def x_src(fc):
                return xprev[:, fc, :]
            retain = None
            src_is_bf = True
        else:
            @blk.vector
            def _(v):
                c.wait(v, "MM", last_bu_mm[0])

            def x_src(fc):
                return c.resid[:, fc, sp * SW:(sp + 1) * SW]
            retain = (afull[:, sp * SW:(sp + 1) * SW],
                      bfull[:, sp * SW:(sp + 1) * SW])
            src_is_bf = False

        def out_bf(fc):
            return xlnb[:, fc, :]

        emit_ln(c, x_src, lscr, out_bf, retain=retain, src_is_bf=src_is_bf)
        vv_ln = c.val("VV")

        @blk.tensor
        def _(t):
            c.wait(t, "VV", vv_ln)
            if half == 0 and sp == 0:
                c.wait(t, "WD", wd_wts)
            for fc in range(FC):
                t.matmul(c.pb[4][0:NST, :], BrT[:, fc, :], xlnb[:, fc, :],
                         start=(fc == 0), stop=(fc == FC - 1))
            for fc in range(FC):
                ins = t.matmul(c.pb[5][0:NST, :], BiT[:, fc, :], xlnb[:, fc, :],
                               start=(fc == 0), stop=(fc == FC - 1))
            c.inc(ins, "MM")
        last_bu_mm[0] = c.val("MM")
        mm_bu = c.val("MM")

        @blk.vector
        def _(v):
            c.wait(v, "MM", mm_bu)
            v.tensor_scalar(U[0:NST, pos:pos + SW], c.pb[4][0:NST, :],
                            dtc[0:NST, :], None, ALU.mult)
            ins = v.tensor_scalar(U[NST:P, pos:pos + SW], c.pb[5][0:NST, :],
                                  dtc[0:NST, :], None, ALU.mult)
            c.inc(ins, "VV")

    for sp in range(NS):
        bu_strip(0, sp)
    for sp in range(NS):
        bu_strip(1, sp)
    vv_bu = c.val("VV")

    QW = T // 2
    for half in range(4):
        hsl = slice(half * QW, (half + 1) * QW)
        vv_pre = c.val("VV")

        @blk.vector
        def _(v):
            c.wait(v, "VV", vv_pre)
            v.tensor_mul(scrA[0:NST, :], ct[:, hsl], U[0:NST, hsl])
            ins = v.tensor_mul(scrA[NST:P, :], ct_hi[:, hsl], U[NST:P, hsl])
            c.inc(ins, "VV")
        vv_ra = c.val("VV")

        @blk.gpsimd
        def _(g):
            c.wait(g, "VV", vv_pre)
            g.tensor_mul(scrB[0:NST, :], st_hi[:, hsl], U[NST:P, hsl])
            ins = g.tensor_mul(scrB[NST:P, :], st[:, hsl], U[0:NST, hsl])
            c.inc(ins, "PP")
        pp_rb = c.val("PP")

        @blk.vector
        def _(v):
            c.wait(v, "PP", pp_rb)
            c.wait(v, "VV", vv_ra)
            v.tensor_add(U[0:NST, hsl], scrA[0:NST, :], scrB[0:NST, :])
            ins = v.tensor_sub(U[NST:P, hsl], scrA[NST:P, :], scrB[NST:P, :])
            c.inc(ins, "VV")
    vv_rot = c.val("VV")

    @blk.vector
    def _(v):
        c.wait(v, "VV", vv_rot)
        mbc = mcol.broadcast_to([P, T])
        ins = v.tensor_tensor_scan(W[:, :], mbc, U[:, 0:T], 0.0,
                                   ALU.mult, ALU.add)
        c.inc(ins, "VV")
        c.wait(v, "VV")
        ins = v.tensor_copy(scrA[:, 0:1], W[:, T - 1:T])
        c.inc(ins, "VV")
        c.wait(v, "VV")
        ins = v.tensor_tensor_scan(W[:, :], mbc, U[:, T:FT], scrA[:, 0:1],
                                   ALU.mult, ALU.add)
        c.inc(ins, "VV")
    vv_scan = c.val("VV")

    for dh in range(2):
        js = slice(T + dh * QW, T + (dh + 1) * QW)
        ws = slice(dh * QW, (dh + 1) * QW)
        vv_pre = c.val("VV")

        @blk.vector
        def _(v):
            c.wait(v, "VV", vv_pre)
            v.tensor_mul(scrA[0:NST, :], ct[:, js], W[0:NST, ws])
            ins = v.tensor_mul(scrA[NST:P, :], ct_hi[:, js], W[NST:P, ws])
            c.inc(ins, "VV")
        vv_da = c.val("VV")

        @blk.gpsimd
        def _(g):
            c.wait(g, "VV", vv_pre)
            g.tensor_mul(scrB[0:NST, :], st_hi[:, js], W[NST:P, ws])
            ins = g.tensor_mul(scrB[NST:P, :], st[:, js], W[0:NST, ws])
            c.inc(ins, "PP")
        pp_db = c.val("PP")

        @blk.vector
        def _(v):
            c.wait(v, "PP", pp_db)
            c.wait(v, "VV", vv_da)
            v.tensor_sub(Sst[0:NST, ws], scrA[0:NST, :], scrB[0:NST, :])
            ins = v.scalar_tensor_tensor(Sst[NST:P, ws], scrB[NST:P, :], -1.0,
                                         scrA[NST:P, :], ALU.mult, ALU.subtract)
            c.inc(ins, "VV")
    vv_sst = c.val("VV")

    mm_wout_prev = [c.val("MM")]
    for sp in range(NS):
        ssl = slice(sp * SW, (sp + 1) * SW)

        pp_x0 = c.val("PP")
        vv_yfc = None

        mm_c0 = c.val("MM")
        vv_c0 = c.val("VV")

        @blk.tensor
        def _(t):
            c.wait(t, "VV", vv_c0)
            for oc in range(FC):
                bank = c.pb[4 + (oc % 2)]
                if oc >= 2:
                    c.wait_fut(t, "VV", vv_c0 + oc - 1)
                ins = t.matmul(bank[:, :], CCw[:, oc * P:(oc + 1) * P],
                               Sst[:, ssl], start=True, stop=True)
                c.inc(ins, "MM")

        @blk.gpsimd
        def _(g):
            c.wait(g, "VV", vv_bu)
            c.wait(g, "LD", ld_small)
            for oc in range(FC):
                if oc >= 2:
                    c.wait_fut(g, "VV", vv_c0 + oc - 1)
                g.tensor_mul(pscr[:, oc % 2, :], c.resid[:, oc, ssl], afull[:, ssl])
                ins = g.tensor_add(pscr[:, oc % 2, :], pscr[:, oc % 2, :], bfull[:, ssl])
                c.inc(ins, "PP")

        @blk.vector
        def _(v):
            c.wait(v, "LD", ld_small)
            c.wait(v, "MM", mm_wout_prev[0])
            for oc in range(FC):
                c.wait(v, "MM", mm_c0 + oc + 1)
                c.wait(v, "PP", pp_x0 + oc + 1)
                ins = v.scalar_tensor_tensor(ybf[:, oc, :], pscr[:, oc % 2, :],
                                             Dcol[:, oc:oc + 1],
                                             c.pb[4 + (oc % 2)][:, :],
                                             ALU.mult, ALU.add)
                c.inc(ins, "VV")
        vv_y = c.val("VV")
        assert vv_y == vv_c0 + FC

        mm_w0 = c.val("MM")

        @blk.tensor
        def _(t):
            c.wait(t, "VV", vv_y)
            for oc in range(FC):
                bank = c.pb[4 + (oc % 2)]
                if oc >= 2:
                    c.wait_fut(t, "VV", vv_y + oc - 1)
                for kc in range(FC):
                    ins = t.matmul(bank[:, :], WoT[:, kc, oc * P:(oc + 1) * P],
                                   ybf[:, kc, :], start=(kc == 0), stop=(kc == FC - 1))
                c.inc(ins, "MM")
        mm_wout_prev[0] = c.val("MM")

        @blk.vector
        def _(v):
            for oc in range(FC):
                bank = c.pb[4 + (oc % 2)]
                c.wait(v, "MM", mm_w0 + oc + 1)
                ins = v.scalar_tensor_tensor(c.resid[:, oc, ssl], bank[:, :],
                                             bout[:, oc:oc + 1],
                                             c.resid[:, oc, ssl], ALU.add, ALU.add)
                c.inc(ins, "VV")





def emit_moe(c, es, pfx, depth_scale):
    nc, blk = c.nc, c.block
    d = c.d
    HB = 512
    NHB = HID // HB
    NBLK = NE * NHB

    XLNB = es.enter_context(nc.sbuf_tensor(pfx + "_xlnb", [P, FC, T], BF16)).ap()
    gate_sb = es.enter_context(nc.sbuf_tensor(pfx + "_gatesb", [P, FC, 8], F32)).ap()
    abtok = es.enter_context(nc.sbuf_tensor(pfx + "_abtok", [P, 2, 16], F32)).ap()
    gs_sb = es.enter_context(nc.sbuf_tensor(pfx + "_gs", [P, 8], F32)).ap()
    gs_row = es.enter_context(nc.sbuf_tensor(pfx + "_gsr", [1, 8], F32)).ap()
    W1B = es.enter_context(nc.sbuf_tensor(pfx + "_w1b", [P, 2, FC, HB], BF16)).ap()
    W3B = es.enter_context(nc.sbuf_tensor(pfx + "_w3b", [P, 2, FC, HB], BF16)).ap()
    W2B = es.enter_context(nc.sbuf_tensor(pfx + "_w2b", [P, 2, 4, 2 * HB], BF16)).ap()
    tokw_em = es.enter_context(nc.sbuf_tensor(pfx + "_tem", [8, T], F32)).ap()
    tokw_bc = es.enter_context(nc.sbuf_tensor(pfx + "_tbc", [P, T], F32)).ap()
    tokw_row = es.enter_context(nc.sbuf_tensor(pfx + "_trow", [1, 2, SW], F32)).ap()
    ltok = es.enter_context(nc.sbuf_tensor(pfx + "_ltok", [P, 16, 8], F32)).ap()
    twk = es.enter_context(nc.sbuf_tensor(pfx + "_twk", [P, 16, 8], F32)).ap()
    t2a = es.enter_context(nc.sbuf_tensor(pfx + "_t2a", [P, 16, 8], F32)).ap()
    t2b = es.enter_context(nc.sbuf_tensor(pfx + "_t2b", [P, 16, 8], F32)).ap()
    t2c = es.enter_context(nc.sbuf_tensor(pfx + "_t2c", [P, 16, 8], F32)).ap()
    cols = es.enter_context(nc.sbuf_tensor(pfx + "_cols", [P, 8, 16], F32)).ap()
    sg = es.enter_context(nc.sbuf_tensor(pfx + "_sg", [P, 2, SW], F32)).ap()
    dtm = es.enter_context(nc.sbuf_tensor(pfx + "_dtm", [P, 2, SW], F32)).ap()
    h2 = es.enter_context(nc.sbuf_tensor(pfx + "_h2", [P, 2, 4, SW], BF16)).ap()
    lscr = {
        "xb": es.enter_context(nc.sbuf_tensor(pfx + "_lxb", [P, FC, SW], BF16)).ap(),
        "sq": es.enter_context(nc.sbuf_tensor(pfx + "_lsq", [P, 2, SW], BF16)).ap(),
        "ab": es.enter_context(nc.sbuf_tensor(pfx + "_lab", [P, 2 * SW], F32)).ap(),
        "lnt": dtm[:, 0, :],
        "rows": es.enter_context(nc.sbuf_tensor(pfx + "_lrows", [1, 4 * SW], F32)).ap(),
    }

    last_mm = [c.val("MM")]
    for sp in range(NS):
        ssl = slice(sp * SW, (sp + 1) * SW)

        @blk.vector
        def _(v, lm=last_mm[0]):
            c.wait(v, "MM", lm)

        def x_src(fc, ssl=ssl):
            return c.resid[:, fc, ssl]

        def out_bf(fc, ssl=ssl):
            return XLNB[:, fc, ssl]

        emit_ln(c, x_src, lscr, out_bf,
                rows_tok=(abtok[:, 0, :], abtok[:, 1, :], sp))
        last_mm[0] = c.val("MM")
    vv_ln = c.val("VV")

    @blk.sync
    def _(s):
        ins = s.dma_start(out=gate_sb,
                          in_=d[pfx + "_gate"].rearrange("(fc p) e -> p fc e", p=P))
        c.inc(ins, "LD", 16)
    wd_gate = c.val("LD")

    mm_lg0 = c.val("MM")

    @blk.tensor
    def _(t):
        c.wait(t, "VV", vv_ln)
        c.wait(t, "LD", wd_gate)
        for sp in range(NS):
            for fc in range(FC):
                ins = t.matmul(c.pb[4][0:8, :], gate_sb[:, fc, :],
                               c.resid[:, fc, sp * SW:(sp + 1) * SW],
                               start=(fc == 0), stop=(fc == FC - 1))
            c.inc(ins, "MM")
            if sp < NS - 1:
                c.wait_fut(t, "VV", vv_ln + sp + 1)
        for fc in range(FC):
            ins = t.matmul(c.pbx[0:1, 0:8], c.ones_f32[0:P, 0:1], gate_sb[:, fc, :],
                           start=(fc == 0), stop=(fc == FC - 1))
        c.inc(ins, "MM")
    mm_gs = c.val("MM")

    @blk.vector
    def _(v):
        for sp in range(NS):
            c.wait(v, "MM", mm_lg0 + sp + 1)
            ins = v.tensor_copy(tokw_em[:, sp * SW:(sp + 1) * SW], c.pb[4][0:8, :])
            c.inc(ins, "VV")
        c.wait(v, "MM", mm_gs)
        ins = v.tensor_copy(gs_row, c.pbx[0:1, 0:8])
        c.inc(ins, "VV")
    vv_em = c.val("VV")

    @blk.tensor
    def _(t):
        c.wait(t, "VV", vv_em)
        ins = t.matmul(c.pbx[:, 0:8], c.ones_f32[0:1, 0:P], gs_row,
                       start=True, stop=True)
        c.inc(ins, "MM")
    mm_gsb = c.val("MM")

    @blk.vector
    def _(v):
        c.wait(v, "MM", mm_gsb)
        ins = v.tensor_copy(gs_sb, c.pbx[:, 0:8])
        c.inc(ins, "VV")
    vv_em = c.val("VV")

    if os.environ.get("KDBG"):
        @blk.gpsimd
        def _(g):
            c.wait(g, "VV", vv_em)
            ins = g.dma_start(out=d["dbg"][0:8, :], in_=tokw_em)
            c.inc(ins, "ST", 16)

    mm_tr0 = c.val("MM")

    @blk.tensor
    def _(t):
        c.wait(t, "VV", vv_em)
        for tt in range(16):
            if tt >= 1:
                c.wait_fut(t, "VV", vv_em + tt)
            ins = t.transpose(c.pb[5][:, 0:8], tokw_em[0:8, tt * P:(tt + 1) * P],
                              c.ident_f32[0:8, 0:8])
            c.inc(ins, "MM")

    @blk.vector
    def _(v):
        for tt in range(16):
            c.wait(v, "MM", mm_tr0 + tt + 1)
            ins = v.tensor_copy(ltok[:, tt, :], c.pb[5][:, 0:8])
            c.inc(ins, "VV")
    vv_ltok = c.val("VV")

    def excl_prefix(v, e_in, out):
        v.tensor_scalar_mul(out[:, :, 0:1], e_in[:, :, 0:1], 0.0)
        v.tensor_copy(out[:, :, 1:8], e_in[:, :, 0:7])
        for k in range(2, 8):
            v.tensor_copy(t2c[:, :, 0:8 - k], e_in[:, :, 0:8 - k])
            v.tensor_add(out[:, :, k:8], out[:, :, k:8], t2c[:, :, 0:8 - k])

    m1 = cols[:, 0, :]
    m2 = cols[:, 1, :]
    ed = cols[:, 2, :]
    w1c = cols[:, 3, :]
    w2c = cols[:, 4, :]

    @blk.vector
    def _(v):
        c.wait(v, "VV", vv_ltok)
        abc = abtok[:, 0:1, :].rearrange("p a w -> p w a").broadcast_to([P, 16, 8])
        bbc = abtok[:, 1:2, :].rearrange("p a w -> p w a").broadcast_to([P, 16, 8])
        gbc = gs_sb.unsqueeze(1).broadcast_to([P, 16, 8])
        v.tensor_mul(t2a, ltok, abc)
        v.tensor_mul(t2b, bbc, gbc)
        ins = v.engine_nop()
        c.inc(ins, "VV")
        c.wait(v, "VV")
        ins = v.tensor_add(ltok, t2a, t2b)
        c.inc(ins, "VV")
        c.wait(v, "VV")
        v.tensor_reduce(cols[:, 0:1, :].rearrange("p a w -> p w a"), ltok,
                        mybir.AxisListType.X, ALU.max)
        ins = v.engine_nop()
        c.inc(ins, "VV")
        c.wait(v, "VV")
        m1b = cols[:, 0:1, :].rearrange("p a w -> p w a").broadcast_to([P, 16, 8])
        v.tensor_tensor(t2a, ltok, m1b, ALU.is_equal)
        excl_prefix(v, t2a, t2b)
        v.tensor_single_scalar(t2b, t2b, 0.0, ALU.is_equal)
        v.tensor_mul(t2a, t2a, t2b)
        ins = v.engine_nop()
        c.inc(ins, "VV")
        c.wait(v, "VV")
        v.tensor_copy(twk, t2a)
        v.tensor_scalar_mul(t2b, t2a, -1e30)
        v.tensor_add(t2b, t2b, ltok)
        ins = v.engine_nop()
        c.inc(ins, "VV")
        c.wait(v, "VV")
        v.tensor_reduce(cols[:, 1:2, :].rearrange("p a w -> p w a"), t2b,
                        mybir.AxisListType.X, ALU.max)
        ins = v.engine_nop()
        c.inc(ins, "VV")
        c.wait(v, "VV")
        m2b = cols[:, 1:2, :].rearrange("p a w -> p w a").broadcast_to([P, 16, 8])
        v.tensor_tensor(t2a, t2b, m2b, ALU.is_equal)
        excl_prefix(v, t2a, t2b)
        v.tensor_single_scalar(t2b, t2b, 0.0, ALU.is_equal)
        v.tensor_mul(t2a, t2a, t2b)
        ins = v.tensor_sub(cols[:, 2, :], cols[:, 1, :], cols[:, 0, :])
        c.inc(ins, "VV")
    vv_t2a = c.val("VV")

    @blk.scalar
    def _(s):
        c.wait(s, "VV", vv_t2a)
        ins = s.activation(cols[:, 2, :], cols[:, 2, :], AF.Exp)
        c.inc(ins, "SS")
    ss_ed = c.val("SS")

    @blk.vector
    def _(v):
        c.wait(v, "SS", ss_ed)
        v.tensor_scalar(cols[:, 5, :], cols[:, 2, :], 1.0, None, ALU.add)
        ins = v.engine_nop()
        c.inc(ins, "VV")
        c.wait(v, "VV")
        ins = v.reciprocal(cols[:, 6, :], cols[:, 5, :])
        c.inc(ins, "VV")
        c.wait(v, "VV")
        v.tensor_scalar_mul(cols[:, 3, :], cols[:, 6, :], float(depth_scale))
        v.tensor_mul(cols[:, 4, :], cols[:, 2, :], cols[:, 6, :])
        ins = v.engine_nop()
        c.inc(ins, "VV")
        c.wait(v, "VV")
        ins = v.tensor_scalar_mul(cols[:, 4, :], cols[:, 4, :], float(depth_scale))
        c.inc(ins, "VV")
        c.wait(v, "VV")
        w1b = cols[:, 3:4, :].rearrange("p a w -> p w a").broadcast_to([P, 16, 8])
        w2b = cols[:, 4:5, :].rearrange("p a w -> p w a").broadcast_to([P, 16, 8])
        v.tensor_mul(t2b, twk, w1b)
        v.tensor_mul(t2c, t2a, w2b)
        ins = v.engine_nop()
        c.inc(ins, "VV")
        c.wait(v, "VV")
        ins = v.tensor_add(twk, t2b, t2c)
        c.inc(ins, "VV")
    vv_twk = c.val("VV")

    mm_tb0 = c.val("MM")

    @blk.tensor
    def _(t):
        c.wait(t, "VV", vv_twk)
        for tt in range(16):
            if tt >= 1:
                c.wait_fut(t, "VV", vv_twk + tt)
            ins = t.transpose(c.pb[5][0:8, 0:P], twk[:, tt, :], c.ident_f32)
            c.inc(ins, "MM")

    @blk.vector
    def _(v):
        for tt in range(16):
            c.wait(v, "MM", mm_tb0 + tt + 1)
            ins = v.tensor_copy(tokw_em[0:8, tt * P:(tt + 1) * P], c.pb[5][0:8, 0:P])
            c.inc(ins, "VV")
    vv_tem = c.val("VV")

    if os.environ.get("KDBG"):
        @blk.gpsimd
        def _(g):
            c.wait(g, "VV", vv_tem)
            c.wait(g, "ST")
            ins = g.dma_start(out=d["dbg"][8:16, :], in_=tokw_em)
            c.inc(ins, "ST", 16)

    def emit_wdma(k, wait_mm):
        e, b = divmod(k, NHB)
        buf = k % 2
        sem = "WD" if k % 2 == 0 else "WD2"

        @blk.sync
        def _(g):
            if wait_mm:
                c.wait_fut(g, "MM", wait_mm)
            else:
                c.wait(g, "BAR")
            ins = g.dma_start(out=W1B[:, buf].rearrange("p a b -> p (a b)"),
                              in_=d[pfx + "_w1"][e, b])
            c.inc(ins, sem, 16)
            ins = g.dma_start(out=W3B[:, buf].rearrange("p a b -> p (a b)"),
                              in_=d[pfx + "_w3"][e, b])
            c.inc(ins, sem, 16)
            ins = g.dma_start(out=W2B[:, buf].rearrange("p a b -> p (a b)"),
                              in_=d[pfx + "_w2"][e, b])
            c.inc(ins, sem, 16)

    wdb = {0: c.val("WD"), 1: c.val("WD2")}
    emit_wdma(0, None)
    emit_wdma(1, None)

    mm_done = {-1: c.val("MM"), -2: c.val("MM")}
    vv_done = {-1: c.val("VV")}

    for k in range(NBLK):
        e, b = divmod(k, NHB)
        buf = k % 2

        if b == 0:
            mm_bc0 = c.val("MM")
            rowb = c.val("ROWD")

            @blk.sync
            def _(g):
                c.wait(g, "VV", vv_tem)
                if k > 0:
                    c.wait(g, "MM", mm_done[k - 1])
                for sp in range(NS):
                    if sp >= 1:
                        c.wait(g, "ROWD", rowb + 16 * sp)
                    if sp >= 2:
                        c.wait_fut(g, "MM", mm_bc0 + sp - 1)
                    ins = g.dma_start(out=tokw_row[:, sp % 2, :],
                                      in_=tokw_em[e:e + 1, sp * SW:(sp + 1) * SW])
                    c.inc(ins, "ROWD", 16)
            wd_row0 = rowb

            vv_bc0 = c.val("VV")

            @blk.tensor
            def _(t):
                c.wait(t, "VV", vv_done[k - 1])
                for sp in range(NS):
                    c.wait(t, "ROWD", wd_row0 + 16 * (sp + 1))
                    if sp >= 2:
                        c.wait_fut(t, "VV", vv_bc0 + sp - 1)
                    ins = t.matmul(c.pb[4 + (sp % 2)][:, :], c.ones_f32[0:1, 0:P],
                                   tokw_row[0:1, sp % 2, :],
                                   start=True, stop=True)
                    c.inc(ins, "MM")
            mm_bc = c.val("MM")

            @blk.vector
            def _(v):
                for sp in range(NS):
                    c.wait(v, "MM", mm_bc - NS + sp + 1)
                    ins = v.tensor_copy(tokw_bc[:, sp * SW:(sp + 1) * SW],
                                        c.pb[4 + (sp % 2)][:, :])
                    c.inc(ins, "VV")

        wd_sem = "WD" if k % 2 == 0 else "WD2"
        wd_need = wdb[k % 2] + 48 * (k // 2 + 1)
        mm0 = c.val("MM")
        vv0 = c.val("VV")
        ss0 = c.val("SS")

        def mm_at(sp, q):
            return mm0 + sp * 16 + q

        def vv_at(sp, q):
            return vv0 + sp * 20 + q

        def ss_at(sp, q):
            return ss0 + sp * 4 + q

        vv_prev = vv_done[k - 1]

        @blk.tensor
        def _(t):
            c.wait(t, wd_sem, wd_need)
            for sp in range(NS):
                ssl = slice(sp * SW, (sp + 1) * SW)
                for m in range(4):
                    q = sp * 4 + m
                    if q >= 2:
                        c.wait_fut(t, "SS", ss0 + q - 1)
                    else:
                        c.wait(t, "SS", ss0 - 1 + q)
                    for fc in range(FC):
                        ins = t.matmul(c.pb[m % 2][:, :],
                                       W1B[:, buf, fc, m * P:(m + 1) * P],
                                       XLNB[:, fc, ssl],
                                       start=(fc == 0), stop=(fc == FC - 1))
                    c.inc(ins, "MM")
                    if q >= 2:
                        qq = q - 2
                        c.wait_fut(t, "VV", vv0 + (qq // 4) * 20 + (qq % 4) + 1)
                    else:
                        c.wait(t, "VV", max(0, vv_prev - 17 + q))
                    for fc in range(FC):
                        ins = t.matmul(c.pb[2 + m % 2][:, :],
                                       W3B[:, buf, fc, m * P:(m + 1) * P],
                                       XLNB[:, fc, ssl],
                                       start=(fc == 0), stop=(fc == FC - 1))
                    c.inc(ins, "MM")
                for oc in range(FC):
                    if oc >= 2:
                        c.wait_fut(t, "VV", vv0 + sp * 20 + 4 + 2 * (oc - 2) + 1)
                    elif sp >= 1:
                        c.wait_fut(t, "VV", vv0 + (sp - 1) * 20 + 4 + 2 * (oc + 6) + 1)
                    elif b == 0:
                        c.wait(t, "VV", vv0)
                    else:
                        c.wait(t, "VV", max(0, vv_prev - 3 + 2 * oc))
                    if oc == 0:
                        for hc in range(4):
                            c.wait_fut(t, "VV", vv0 + sp * 20 + hc + 1)
                            ins = t.matmul(c.pb[4][:, :],
                                           W2B[:, buf, hc, 0:P],
                                           h2[:, sp % 2, hc, :],
                                           start=(hc == 0), stop=(hc == 3))
                    else:
                        for hc in range(4):
                            ins = t.matmul(c.pb[4 + (oc % 2)][:, :],
                                           W2B[:, buf, hc, oc * P:(oc + 1) * P],
                                           h2[:, sp % 2, hc, :],
                                           start=(hc == 0), stop=(hc == 3))
                    c.inc(ins, "MM")
        mm_done[k] = c.val("MM")
        assert mm_done[k] == mm0 + NS * 16

        @blk.scalar
        def _(s):
            for sp in range(NS):
                for m in range(4):
                    q = sp * 4 + m
                    c.wait_fut(s, "MM", mm_at(sp, 2 * m + 1))
                    if q >= 2:
                        qq = q - 2
                        c.wait_fut(s, "VV", vv_at(qq // 4, qq % 4) + 1)
                    else:
                        c.wait(s, "VV", max(0, vv_prev - 17 + q))
                    ins = s.activation(sg[:, m % 2, :], c.pb[m % 2][:, :], AF.Silu)
                    c.inc(ins, "SS")
        assert c.val("SS") == ss0 + NS * 4

        @blk.vector
        def _(v):
            for sp in range(NS):
                ssl = slice(sp * SW, (sp + 1) * SW)
                for m in range(4):
                    q = sp * 4 + m
                    c.wait_fut(v, "SS", ss0 + q + 1)
                    c.wait_fut(v, "MM", mm_at(sp, 2 * m + 2))
                    if sp >= 2:
                        c.wait_fut(v, "MM", mm_at(sp - 2, 16))
                    ins = v.tensor_mul(h2[:, sp % 2, m, :], sg[:, m % 2, :],
                                       c.pb[2 + m % 2][:, :])
                    c.inc(ins, "VV")
                for oc in range(FC):
                    c.wait_fut(v, "MM", mm_at(sp, 8 + oc + 1))
                    ins = v.tensor_mul(dtm[:, oc % 2, :], c.pb[4 + (oc % 2)][:, :],
                                       tokw_bc[:, ssl])
                    c.inc(ins, "VV")
                    ins = v.tensor_tensor(c.resid[:, oc, ssl], dtm[:, oc % 2, :],
                                          c.resid[:, oc, ssl], ALU.add)
                    c.inc(ins, "VV")
        vv_done[k] = c.val("VV")
        assert vv_done[k] == vv0 + NS * 20

        if k + 2 < NBLK:
            emit_wdma(k + 2, mm_done[k])




SWAP_MASK = [i ^ 1 for i in range(32)]


def emit_attn(c, es):
    nc, blk = c.nc, c.block
    d = c.d
    NQ = FT // SW
    c.attn_proj_mm = {}
    c.attn_oproj_mm = {}

    ropeC = es.enter_context(nc.sbuf_tensor("a_ropec", [P, FT], BF16)).ap()
    ropeS = es.enter_context(nc.sbuf_tensor("a_ropes", [P, FT], BF16)).ap()
    masks = es.enter_context(nc.sbuf_tensor("a_masks", [P, 4, SW], BF16)).ap()
    wq = es.enter_context(nc.sbuf_tensor("a_wq", [P, FC, P], BF16)).ap()
    wk = es.enter_context(nc.sbuf_tensor("a_wk", [P, FC, P], BF16)).ap()
    wv = es.enter_context(nc.sbuf_tensor("a_wv", [P, FC, P], BF16)).ap()
    xps = es.enter_context(nc.sbuf_tensor("a_xps", [P, 2, FC, SW], BF16)).ap()
    xat = xps
    QT = es.enter_context(nc.sbuf_tensor("a_qt", [P, FT], BF16)).ap()
    KT = es.enter_context(nc.sbuf_tensor("a_kt", [P, FT], BF16)).ap()
    Vtok = es.enter_context(nc.sbuf_tensor("a_vtok", [P, 32, 130], BF16)).ap()
    OT = es.enter_context(nc.sbuf_tensor("a_ot", [P, FT], BF16)).ap()
    qr = es.enter_context(nc.sbuf_tensor("a_qr", [P, SW], F32)).ap()
    qsh = es.enter_context(nc.sbuf_tensor("a_qsh", [P, SW], F32)).ap()
    rope_t = es.enter_context(nc.sbuf_tensor("a_ropet", [P, 2, SW], F32)).ap()
    vsb = es.enter_context(nc.sbuf_tensor("a_vsb", [P, SW], BF16)).ap()
    qsq = es.enter_context(nc.sbuf_tensor("a_qsq", [P, SW], BF16)).ap()
    PT = es.enter_context(nc.sbuf_tensor("a_pt", [P, 8, SW], BF16)).ap()
    mx = es.enter_context(nc.sbuf_tensor("a_mx", [1, 2, 2, NQ], F32)).ap()
    rowsc = es.enter_context(nc.sbuf_tensor("a_rows", [1, 8], F32)).ap()
    negB = es.enter_context(nc.sbuf_tensor("a_negb", [P, 2], F32)).ap()
    rb = es.enter_context(nc.sbuf_tensor("a_rb", [NST, SW], F32)).ap()
    rrow = es.enter_context(nc.sbuf_tensor("a_rrow", [1, 2, SW], F32)).ap()
    tok0 = es.enter_context(nc.sbuf_tensor("a_tok0", [1, 1], I32)).ap()
    ksq = es.enter_context(nc.sbuf_tensor("a_ksq", [P, SW], BF16)).ap()

    @blk.sync
    def _(s):
        s.dma_start(out=ropeC, in_=d["rope_cos"]).then_inc(c.sems["LD"], 16)
        s.dma_start(out=ropeS, in_=d["rope_sin"]).then_inc(c.sems["LD"], 16)
        s.dma_start(out=masks, in_=d["attn_masks"].rearrange("p (a w) -> p a w", a=4)
                    ).then_inc(c.sems["LD"], 16)
        s.dma_start(out=tok0, in_=d["tok0"]).then_inc(c.sems["LD"], 16)
        c.cnt["LD"] += 64
    ld_stat = c.val("LD")

    @blk.vector
    def _(v):
        v.tensor_copy(Vtok[:, :, 64:65], c.ones_bf[:, 0:32].rearrange(
            "p (a b) -> p a b", b=1))
        ins = v.tensor_copy(Vtok[:, :, 129:130], c.ones_bf[:, 0:32].rearrange(
            "p (a b) -> p a b", b=1))
        c.inc(ins, "VV")
    vv_vones = c.val("VV")

    es2 = ExitStack()
    xlns = es2.enter_context(nc.sbuf_tensor("a_xlns", [P, 2, FC, SW], BF16)).ap()
    lscr = {
        "xb": es2.enter_context(nc.sbuf_tensor("a_lxb", [P, FC, SW], BF16)).ap(),
        "sq": es2.enter_context(nc.sbuf_tensor("a_lsq", [P, 2, SW], BF16)).ap(),
        "ab": es2.enter_context(nc.sbuf_tensor("a_lab", [P, 2 * SW], F32)).ap(),
        "lnt": es2.enter_context(nc.sbuf_tensor("a_lnt", [P, SW], F32)).ap(),
        "rows": es2.enter_context(nc.sbuf_tensor("a_lrows", [1, 4 * SW], F32)).ap(),
    }
    last_mm = [c.val("MM")]
    for sp in range(NS):
        ssl = slice(sp * SW, (sp + 1) * SW)

        @blk.vector
        def _(v, lm=last_mm[0]):
            c.wait(v, "MM", lm)
            if sp >= 2:
                c.wait(v, "XC", 16 * (sp - 1))

        def x_src(fc, ssl=ssl):
            return c.resid[:, fc, ssl]

        def out_bf(fc, sp=sp):
            return xlns[:, sp % 2, fc, :]

        emit_ln(c, x_src, lscr, out_bf)
        last_mm[0] = c.val("MM")
        vv_lnsp = c.val("VV")

        @blk.gpsimd
        def _(g, sp=sp, ssl=ssl, vv_lnsp=vv_lnsp):
            c.wait(g, "VV", vv_lnsp)
            if sp >= 1:
                c.wait(g, "XC", 16 * sp)
            ins = g.dma_start(
                out=d["ccx_in"][:, ssl].rearrange("(fc p) w -> p fc w", p=P),
                in_=xlns[:, sp % 2])
            c.inc(ins, "XC", 16)

    @blk.gpsimd
    def _(g):
        c.wait(g, "XC")
        ins = g.collective_compute(
            "AllGather", ALU.bypass,
            replica_groups=[[0, 1], [2, 3], [4, 5], [6, 7]],
            ins=[d["ccx_in"].opt()], outs=[d["ccx_out"].opt()])
        c.inc(ins, "XC", 1)
    xc_x = c.val("XC")
    es2.close()

    for hp in range(4):
        hp_mm0 = c.val("MM")

        @blk.sync
        def _(g):
            if hp > 0:
                c.wait(g, "MM", hp_mm0)
            for name, tile in (("attn_WqT", wq), ("attn_WkT", wk), ("attn_WvT", wv)):
                ins = g.dma_start(out=tile.rearrange("p a b -> p (a b)"),
                                  in_=d[name][hp])
                c.inc(ins, "WD", 16)
        wd_w = c.val("WD")

        for qs in range(NQ):
            gsl = slice(qs * SW, (qs + 1) * SW)
            par = qs % 2

            rank = qs // (NQ // 2)
            lsl = slice((qs % (NQ // 2)) * SW, (qs % (NQ // 2) + 1) * SW)

            xsem = "LDX" if par == 0 else "LDX2"

            @blk.sync
            def _(g, rank=rank, lsl=lsl, xsem=xsem):
                c.wait(g, "XC", xc_x)
                if qs >= 2:
                    c.wait(g, "MM", c.attn_proj_mm.get((hp, qs - 2), 0))
                elif hp > 0:
                    c.wait(g, "MM", c.attn_proj_mm[(hp - 1, qs + 6)])
                ins = g.dma_start(
                    out=xps[:, par],
                    in_=d["ccx_out"][rank * DM:(rank + 1) * DM, lsl].rearrange(
                        "(fc p) w -> p fc w", p=P))
                c.inc(ins, xsem, 16)
            ld_xp = c.val(xsem)

            mmp0 = c.val("MM")
            vv_prev_strip = c.val("VV")

            @blk.tensor
            def _(t):
                c.wait(t, xsem, ld_xp)
                c.wait(t, "WD", wd_w)
                c.wait(t, "VV", vv_prev_strip)
                if qs == 0 and hp == 0:
                    c.wait(t, "VV", vv_vones)
                for w_t, bank in ((wq, 0), (wk, 1), (wv, 2)):
                    for fc in range(FC):
                        ins = t.matmul(c.pb[bank][:, :], w_t[:, fc, :],
                                       xps[:, par, fc, :],
                                       start=(fc == 0), stop=(fc == FC - 1))
                    c.inc(ins, "MM")
            mm_qkv = c.val("MM")
            c.attn_proj_mm[(hp, qs)] = mm_qkv

            for which, bank, dst in (("q", 0, QT), ("k", 1, KT)):
                vv_r0 = c.val("VV")

                @blk.vector
                def _(v, bank=bank, dst=dst, vv_r0=vv_r0):
                    c.wait(v, "MM", mm_qkv)
                    v.tensor_copy(qr, c.pb[bank][:, :])
                    ins = v.stream_shuffle(qsh, qr, SWAP_MASK)
                    c.inc(ins, "VV")
                pp_r0 = c.val("PP")

                @blk.gpsimd
                def _(g, vv_r0=vv_r0):
                    c.wait(g, "LD", ld_stat)
                    c.wait(g, "VV", vv_r0 + 1)
                    g.tensor_mul(rope_t[:, 0, :], qr, ropeC[:, gsl])
                    ins = g.tensor_mul(rope_t[:, 1, :], qsh, ropeS[:, gsl])
                    c.inc(ins, "PP")

                @blk.vector
                def _(v, dst=dst, pp_r0=pp_r0):
                    c.wait(v, "PP", pp_r0 + 1)
                    ins = v.tensor_add(dst[:, gsl], rope_t[:, 0, :], rope_t[:, 1, :])
                    c.inc(ins, "VV")
            vv_rope = c.val("VV")

            @blk.vector
            def _(v):
                c.wait(v, "MM", mm_qkv)
                ins = v.tensor_copy(vsb, c.pb[2][:, :])
                c.inc(ins, "VV")
            vv_vsb = c.val("VV")

            mmt0 = c.val("MM")

            @blk.tensor
            def _(t):
                c.wait(t, "VV", vv_vsb)
                for j in range(4):
                    if j >= 1:
                        c.wait_fut(t, "VV", vv_vsb + j)
                    ins = t.transpose(c.pbt_bf[:, 0:P], vsb[:, j * P:(j + 1) * P],
                                      c.ident_bf)
                    c.inc(ins, "MM")

            @blk.vector
            def _(v):
                for j in range(4):
                    tt = qs * 4 + j
                    c.wait(v, "MM", mmt0 + j + 1)
                    v.tensor_copy(Vtok[:, tt, 0:64], c.pbt_bf[:, 0:64])
                    ins = v.tensor_copy(Vtok[:, tt, 65:129], c.pbt_bf[:, 64:128])
                    c.inc(ins, "VV")

            ss_b0 = c.val("SS")

            @blk.scalar
            def _(s):
                c.wait(s, "VV", vv_rope)
                ins = s.activation(qsq, QT[:, gsl], AF.Square)
                c.inc(ins, "SS")
                ins = s.activation(ksq, KT[:, gsl], AF.Square)
                c.inc(ins, "SS")

            mmb0 = c.val("MM")
            vvb0 = c.val("VV")

            @blk.tensor
            def _(t):
                for i, sqsrc in ((0, qsq), (1, ksq)):
                    c.wait(t, "SS", ss_b0 + i + 1)
                    for hd in range(2):
                        j = i * 2 + hd
                        if j > 0:
                            c.wait_fut(t, "VV", vvb0 + j)
                        ins = t.matmul(c.pbx[0:1, :],
                                       c.ones_bf[hd * NST:(hd + 1) * NST, 0:1],
                                       sqsrc[hd * NST:(hd + 1) * NST, :],
                                       start=True, stop=True)
                        c.inc(ins, "MM")

            @blk.vector
            def _(v):
                for i in range(2):
                    for hd in range(2):
                        j = i * 2 + hd
                        c.wait(v, "MM", mmb0 + j + 1)
                        ins = v.reduce_max(mx[0:1, i, hd, qs:qs + 1], c.pbx[0:1, :],
                                           mybir.AxisListType.X)
                        c.inc(ins, "VV")

        vv_mx = c.val("VV")
        ss_nb0 = c.val("SS")

        @blk.vector
        def _(v):
            c.wait(v, "VV", vv_mx)
            for hd in range(2):
                v.reduce_max(rowsc[0:1, hd:hd + 1], mx[0:1, 0, hd, :],
                             mybir.AxisListType.X)
                ins = v.reduce_max(rowsc[0:1, 2 + hd:3 + hd], mx[0:1, 1, hd, :],
                                   mybir.AxisListType.X)
            c.inc(ins, "VV")
            c.wait(v, "VV")
            v.tensor_mul(rowsc[0:1, 4:5], rowsc[0:1, 0:1], rowsc[0:1, 2:3])
            ins = v.tensor_mul(rowsc[0:1, 5:6], rowsc[0:1, 1:2], rowsc[0:1, 3:4])
            c.inc(ins, "VV")
        vv_nb = c.val("VV")

        @blk.scalar
        def _(s):
            c.wait(s, "VV", vv_nb)
            ins = s.activation(rowsc[0:1, 6:8], rowsc[0:1, 4:6], AF.Sqrt,
                               bias=c.cvec[0:1, 6:7])
            c.inc(ins, "SS")
        ss_nb = c.val("SS")

        @blk.vector
        def _(v):
            c.wait(v, "SS", ss_nb)
            ins = v.tensor_scalar_mul(rowsc[0:1, 6:8], rowsc[0:1, 6:8],
                                      -0.125 * 1.02)
            c.inc(ins, "VV")
        vv_nb2 = c.val("VV")

        mm_nb0 = c.val("MM")

        @blk.tensor
        def _(t):
            c.wait(t, "VV", vv_nb2)
            for hd in range(2):
                ins = t.matmul(c.pbx[:, hd:hd + 1], c.ones_f32[0:1, 0:P],
                               rowsc[0:1, 6 + hd:7 + hd], start=True, stop=True)
                c.inc(ins, "MM")

        @blk.vector
        def _(v):
            c.wait(v, "MM", mm_nb0 + 2)
            c.wait(v, "SS")
            ins = v.tensor_copy(negB, c.pbx[:, 0:2])
            c.inc(ins, "VV")
        vv_negb = c.val("VV")

        vv_oev_last = [c.val("VV")]
        for qpair in ((0, 1), (2, 3), (4, 5), (6, 7)):
            chains = []
            for ci, qs in enumerate(qpair):
                chains.append({
                    "qs": qs, "nkt": (qs + 1) * 4,
                    "gsl": slice(qs * SW, (qs + 1) * SW),
                    "st": (c.pb[0 + 2 * ci], c.pb[1 + 2 * ci]),
                    "o": ((c.pb[4], c.pb[5]) if ci == 0 else (c.pbx, c.pbt)),
                    "pts": (0 + 2 * ci, 1 + 2 * ci),
                    "ptm": (4 + 2 * ci, 5 + 2 * ci),
                })

            seq = []
            mx_nkt = max(ch["nkt"] for ch in chains)
            for i in range(mx_nkt):
                for ch in chains:
                    if i < ch["nkt"]:
                        seq.append((ch, "S", i))
                for ch in chains:
                    if 0 <= i - 1 < ch["nkt"] - 1:
                        seq.append((ch, "AV", i - 1))
            for ch in chains:
                seq.append((ch, "AV", ch["nkt"] - 1))

            mm0 = c.val("MM")
            ss0 = c.val("SS")
            pp0 = c.val("PP")
            mm_of = {}
            pos = 0
            for ch, kind, kt in seq:
                pos += 2
                mm_of[(ch["qs"], kind, kt)] = mm0 + pos
            ss_of = {}
            scnt = 0
            for ch, kind, kt in seq:
                if kind == "S":
                    scnt += 2
                    ss_of[(ch["qs"], kt)] = ss0 + scnt
            pp_of = {}
            pcnt = 0
            for ch, kind, kt in seq:
                if kind == "S" and (kt * P) >= ch["qs"] * SW:
                    pcnt += 2
                    pp_of[(ch["qs"], kt)] = pp0 + pcnt

            @blk.tensor
            def _(t):
                c.wait(t, "VV", vv_negb)
                c.wait(t, "VV", vv_oev_last[0])
                for ch, kind, kt in seq:
                    qs = ch["qs"]
                    ksl = slice(kt * P, (kt + 1) * P)
                    if kind == "S":
                        if kt >= 1:
                            c.wait_fut(t, "SS", ss_of[(qs, kt - 1)])
                        t.matmul(ch["st"][0][:, :], KT[0:NST, ksl],
                                 QT[0:NST, ch["gsl"]],
                                 start=True, stop=True, tile_position=(0, 0))
                        ins = t.matmul(ch["st"][1][:, :], KT[NST:P, ksl],
                                       QT[NST:P, ch["gsl"]],
                                       start=True, stop=True, tile_position=(64, 0))
                        c.inc(ins, "MM", 2)
                    else:
                        masked = (kt * P) >= qs * SW
                        if masked:
                            c.wait_fut(t, "PP", pp_of[(qs, kt)])
                            sa, sb = ch["ptm"]
                        else:
                            c.wait_fut(t, "SS", ss_of[(qs, kt)])
                            sa, sb = ch["pts"]
                        t.matmul(ch["o"][0][0:65, :], Vtok[:, kt, 0:65],
                                 PT[:, sa, :],
                                 start=(kt == 0), stop=(kt == ch["nkt"] - 1))
                        ins = t.matmul(ch["o"][1][0:65, :], Vtok[:, kt, 65:130],
                                       PT[:, sb, :],
                                       start=(kt == 0), stop=(kt == ch["nkt"] - 1))
                        c.inc(ins, "MM", 2)

            @blk.scalar
            def _(s):
                for ch, kind, kt in seq:
                    if kind != "S":
                        continue
                    qs = ch["qs"]
                    if kt >= 1:
                        c.wait_fut(s, "MM", mm_of[(qs, "AV", kt - 1)])
                    c.wait_fut(s, "MM", mm_of[(qs, "S", kt)])
                    s.activation(PT[:, ch["pts"][0], :], ch["st"][0][:, :],
                                 AF.Exp, bias=negB[:, 0:1], scale=c.cvec[:, 2:3])
                    ins = s.activation(PT[:, ch["pts"][1], :], ch["st"][1][:, :],
                                       AF.Exp, bias=negB[:, 1:2], scale=c.cvec[:, 2:3])
                    c.inc(ins, "SS", 2)

            @blk.gpsimd
            def _(g):
                for ch, kind, kt in seq:
                    qs = ch["qs"]
                    if kind != "S" or (kt * P) < qs * SW:
                        continue
                    mi = (kt * P - qs * SW) // P
                    c.wait_fut(g, "SS", ss_of[(qs, kt)])
                    if kt >= 1:
                        c.wait_fut(g, "MM", mm_of[(qs, "AV", kt - 1)])
                    g.tensor_mul(PT[:, ch["ptm"][0], :], PT[:, ch["pts"][0], :],
                                 masks[:, mi, :])
                    ins = g.tensor_mul(PT[:, ch["ptm"][1], :],
                                       PT[:, ch["pts"][1], :], masks[:, mi, :])
                    c.inc(ins, "PP", 2)

            vv_rc0 = c.val("VV")
            mm_rb0 = c.val("MM")

            @blk.vector
            def _(v):
                if qpair[0] == 0 and hp > 0:
                    c.wait(v, "XC", xc_x + 16 * hp)
                for ci, ch in enumerate(chains):
                    c.wait_fut(v, "MM", mm_of[(ch["qs"], "AV", ch["nkt"] - 1)])
                    v.reciprocal(rrow[:, 0, :], ch["o"][0][NST:NST + 1, :])
                    ins = v.reciprocal(rrow[:, 1, :], ch["o"][1][NST:NST + 1, :])
                    c.inc(ins, "VV")
                    for hd in range(2):
                        c.wait_fut(v, "MM", mm_rb0 + ci * 2 + hd + 1)
                        ins = v.tensor_copy(rb, c.pb[0][0:NST, :])
                        c.inc(ins, "VV")
                        ins = v.tensor_tensor(
                            OT[hd * NST:(hd + 1) * NST, ch["gsl"]],
                            ch["o"][hd][0:NST, :], rb, ALU.mult)
                        c.inc(ins, "VV")

            @blk.tensor
            def _(t):
                free_at = [None, vv_rc0 + 2, vv_rc0 + 4, vv_rc0 + 7]
                for ci in range(2):
                    for hd in range(2):
                        j = ci * 2 + hd
                        c.wait_fut(t, "VV", vv_rc0 + ci * 5 + 1)
                        if j >= 1:
                            c.wait_fut(t, "VV", free_at[j])
                        ins = t.matmul(c.pb[0][0:NST, :], c.ones_f32[0:1, 0:NST],
                                       rrow[0:1, hd, :], start=True, stop=True)
                        c.inc(ins, "MM")
            assert c.val("MM") == mm_rb0 + 4
            vv_oev_last[0] = c.val("VV")

        vv_pair = c.val("VV")

        @blk.gpsimd
        def _(g):
            c.wait(g, "VV", vv_pair)
            ins = g.dma_start(out=d["cco_in"][hp * P:(hp + 1) * P, :], in_=OT)
            c.inc(ins, "XC", 16)

    @blk.gpsimd
    def _(g):
        c.wait(g, "XC")
        ins = g.collective_compute(
            "AllGather", ALU.bypass,
            replica_groups=[[0, 1], [2, 3], [4, 5], [6, 7]],
            ins=[d["cco_in"].opt()], outs=[d["cco_out"].opt()])
        c.inc(ins, "XC", 1)
    xc_o = c.val("XC")

    wo = es.enter_context(nc.sbuf_tensor("a_wo", [P, FC, DM], BF16)).ap()

    mm_gate_wo = c.val("MM")

    @blk.sync
    def _(g):
        c.wait(g, "MM", mm_gate_wo)
        ins = g.dma_start(out=wo.rearrange("p a b -> p (a b)"), in_=d["attn_WoT"])
        c.inc(ins, "WD", 16)
    wd_wo = c.val("WD")

    for sp in range(NS):
        ssl = slice(sp * SW, (sp + 1) * SW)
        par = sp % 2

        xsem = "LDX" if par == 0 else "LDX2"

        @blk.sync
        def _(g, xsem=xsem):
            c.wait(g, "XC", xc_o)
            c.wait(g, "LD", ld_stat)
            if sp >= 2:
                c.wait(g, "MM", c.attn_oproj_mm.get(sp - 2, 0))
            r = g.value_load(tok0)
            src = d["cco_out"][:, sp * SW:(sp + 1) * SW].rearrange(
                "(fc p) w -> p fc w", p=P)
            ins = g.dma_start(out=xat[:, par], in_=dynoff(src, r))
            c.inc(ins, xsem, 16)
        ld_at = c.val(xsem)

        mmo0 = c.val("MM")
        vv_oe0 = c.val("VV")

        ss_cur = c.val("SS")

        @blk.tensor
        def _(t):
            c.wait(t, xsem, ld_at)
            c.wait(t, "WD", wd_wo)
            c.wait(t, "SS", ss_cur)
            for oc in range(FC):
                bank = c.pb[0 + (oc % 2)]
                if oc >= 2:
                    c.wait_fut(t, "VV", vv_oe0 + oc - 1)
                for fc in range(FC):
                    ins = t.matmul(bank[:, :], wo[:, fc, oc * P:(oc + 1) * P],
                                   xat[:, par, fc, :],
                                   start=(fc == 0), stop=(fc == FC - 1))
                c.inc(ins, "MM")
        c.attn_oproj_mm[sp] = c.val("MM")

        @blk.vector
        def _(v):
            for oc in range(FC):
                bank = c.pb[0 + (oc % 2)]
                c.wait(v, "MM", mmo0 + oc + 1)
                ins = v.tensor_tensor(c.resid[:, oc, ssl], bank[:, :],
                                      c.resid[:, oc, ssl], ALU.add)
                c.inc(ins, "VV")


PHASES_DEFAULT = "ssm,moe0,attn,moe1"


def build_program(phases):
    nc = bass.Bass("TRN2", target_bir_lowering=False)
    d = {}

    def din(name, shape, dtype=F32):
        d[name] = nc.dram_tensor(name, shape, dtype, kind="ExternalInput").ap()

    din("xT_own", [DM, T])
    din("xT_prev", [P, FC, T], BF16)
    din("ones_f32", [P, P])
    din("cvec", [P, 8])
    din("ones_bf", [P, P], BF16)
    din("ident_f32", [P, P])
    din("ident_bf", [P, P], BF16)
    d["outT"] = nc.dram_tensor("outT", [DM, T], F32, kind="ExternalOutput").ap()
    if os.environ.get("KDBG"):
        d["dbg"] = nc.dram_tensor("dbg", [64, T], F32, kind="ExternalOutput").ap()
    if os.environ.get("KDBG2") or os.environ.get("KDBG3"):
        d["dbg2"] = nc.dram_tensor("dbg2", [P, 4 * T], F32, kind="ExternalOutput").ap()
    if "ssm" in phases:
        din("ssm_logLam", [NST, 1])
        din("ssm_LamIm", [NST, 1])
        din("ssm_logdt", [1, 1])
        din("ssm_BrT", [P, FC * NST], BF16)
        din("ssm_BiT", [P, FC * NST], BF16)
        din("ssm_CC", [P, DM], BF16)
        din("ssm_WoutT", [P, FC * DM], BF16)
        din("ssm_D", [P, FC])
        din("ssm_bout", [P, FC])
    if "attn" in phases:
        din("rope_cos", [P, FT], BF16)
        din("rope_sin", [P, FT], BF16)
        din("attn_masks", [P, 4 * SW], BF16)
        din("tok0", [1, 1], I32)
        din("attn_WqT", [4, P, FC * P], BF16)
        din("attn_WkT", [4, P, FC * P], BF16)
        din("attn_WvT", [4, P, FC * P], BF16)
        din("attn_WoT", [P, FC * DM], BF16)
        d["ccx_in"] = nc.dram_tensor("ccx_in", [DM, T], BF16, kind="Internal").ap()
        d["ccx_out"] = nc.dram_tensor("ccx_out", [2 * DM, T], BF16, kind="Internal").ap()
        d["cco_in"] = nc.dram_tensor("cco_in", [4 * P, FT], BF16, kind="Internal").ap()
        d["cco_out"] = nc.dram_tensor("cco_out", [DM, FT], BF16, kind="Internal").ap()
    for pfx in ("moe0", "moe1"):
        if pfx in phases:
            din(pfx + "_gate", [DM, NE])
            din(pfx + "_w1", [NE, HID // 512, P, FC * 512], BF16)
            din(pfx + "_w3", [NE, HID // 512, P, FC * 512], BF16)
            din(pfx + "_w2", [NE, HID // 512, P, 4 * DM], BF16)

    es = ExitStack()
    sems = {name: es.enter_context(nc.semaphore(name)) for name in SEM_NAMES}
    resid = es.enter_context(nc.sbuf_tensor("resid", [P, FC, T], F32))
    ones_f32 = es.enter_context(nc.sbuf_tensor("ones_f32s", [P, P], F32))
    cvec = es.enter_context(nc.sbuf_tensor("cvecs", [P, 8], F32))
    ident_f32 = es.enter_context(nc.sbuf_tensor("ident_f32s", [P, P], F32))
    ident_bf_s = es.enter_context(nc.sbuf_tensor("ident_bfs", [P, P], BF16))
    ones_bf = es.enter_context(nc.sbuf_tensor("ones_bfs", [P, P], BF16))
    pb = [es.enter_context(nc.psum_tensor(f"pb{i}", [P, SW], F32)) for i in range(6)]
    pbt = es.enter_context(nc.psum_tensor("pbt", [P, SW], F32))
    pbx = es.enter_context(nc.psum_tensor("pbx", [P, SW], F32))

    block = es.enter_context(nc.Block())
    c = Ctx(nc, block, sems)
    c.d = d
    c.resid = resid.ap()
    c.ones_f32 = ones_f32.ap()
    c.cvec = cvec.ap()
    c.ident_f32 = ident_f32.ap()
    c.ident_bf = ident_bf_s.ap()
    c.ones_bf = ones_bf.ap()
    c.pb = [b.ap() for b in pb]
    c.pbt = pbt.ap()
    c.pbt_bf = pbt.ap().bitcast(BF16)
    c.pbx = pbx.ap()

    @block.sync
    def _(s):
        ins = s.dma_start(out=c.resid,
                          in_=d["xT_own"].rearrange("(fc p) t -> p fc t", p=P))
        c.inc(ins, "LD", 16)
        ins = s.dma_start(out=c.ones_f32, in_=d["ones_f32"])
        c.inc(ins, "LD", 16)
        ins = s.dma_start(out=c.ones_bf, in_=d["ones_bf"])
        c.inc(ins, "LD", 16)
        ins = s.dma_start(out=c.cvec, in_=d["cvec"])
        c.inc(ins, "LD", 16)
        ins = s.dma_start(out=c.ident_f32, in_=d["ident_f32"])
        c.inc(ins, "LD", 16)
        ins = s.dma_start(out=c.ident_bf, in_=d["ident_bf"])
        c.inc(ins, "LD", 16)
    ld_init = c.val("LD")

    for name in ("tensor", "vector", "scalar", "gpsimd"):
        def body(eng, v=ld_init):
            eng.wait_ge(sems["LD"], v)
        getattr(block, name)(body)

    if "ssm" in phases:
        phase_es = ExitStack()
        emit_ssm(c, phase_es)
        emit_barrier(c)
        phase_es.close()
    if "moe0" in phases:
        phase_es = ExitStack()
        emit_moe(c, phase_es, "moe0", 1.0)
        emit_barrier(c)
        phase_es.close()
    if "attn" in phases:
        phase_es = ExitStack()
        emit_attn(c, phase_es)
        emit_barrier(c)
        phase_es.close()
    if "moe1" in phases:
        phase_es = ExitStack()
        emit_moe(c, phase_es, "moe1", DEPTH_SCALE)
        emit_barrier(c)
        phase_es.close()

    @block.gpsimd
    def _(g):
        ins = g.dma_start(out=d["outT"].rearrange("(fc p) t -> p fc t", p=P),
                          in_=c.resid)
        c.inc(ins, "ST", 16)
        c.wait(g, "ST")

    es.close()
    return nc



def w_up(w):
    w = np.asarray(w)
    a = w.reshape(NE, FC, P, HID // 512, 512).transpose(0, 3, 2, 1, 4)
    return np.ascontiguousarray(a.reshape(NE, HID // 512, P, FC * 512)).astype(bfd)


def w_dn(w):
    w = np.asarray(w)
    a = w.reshape(NE, HID // 512, 4, P, DM).transpose(0, 1, 3, 2, 4)
    return np.ascontiguousarray(a.reshape(NE, HID // 512, P, 4 * DM)).astype(bfd)


def w_qkv(w, h):
    a = np.asarray(w).T[:, h * 512:(h + 1) * 512]
    a = a.reshape(FC, P, 4, P).transpose(2, 1, 0, 3)
    return np.ascontiguousarray(a.reshape(4, P, FC * P)).astype(bfd)


def prep_inputs(inputs, phases):
    x = np.asarray(inputs["L0_attn_x"])
    ones_f32 = np.ones((P, P), dtype=np.float32)
    ones_bf = np.ones((P, P), dtype=bfd)
    ident_f32 = np.eye(P, dtype=np.float32)
    ident_bf = np.eye(P).astype(bfd)
    invfreq = 1.0 / (10000.0 ** (np.arange(0, 64, 2) / 64.0))
    dd = np.arange(64)
    ang = np.arange(FT)[None, :] * invfreq[dd % 32][:, None]
    cos_t = np.cos(ang)
    sin_t = np.sin(ang)
    sin_sgn = np.where((dd % 2 == 0)[:, None], -sin_t, sin_t)
    rope_cos = np.ascontiguousarray(np.tile(cos_t, (2, 1))).astype(bfd)
    rope_sin = np.ascontiguousarray(np.tile(sin_sgn, (2, 1))).astype(bfd)
    attn_masks_f = np.zeros((P, 4, SW), dtype=np.float32)
    for mi in range(4):
        attn_masks_f[:, mi, :] = (np.arange(P)[:, None] + mi * P
                                  <= np.arange(SW)[None, :])
    attn_masks = np.ascontiguousarray(attn_masks_f.reshape(P, 4 * SW)).astype(bfd)
    cv = np.zeros((P, 8), dtype=np.float32)
    cv[:, 0] = np.pi / 2
    cv[:, 1] = 1e-5
    cv[:, 2] = 0.125
    cv[:, 3] = -1.0
    cv[:, 4] = -0.5
    cv[:, 5] = 0.5
    maps = []
    for core in range(8):
        b, h = core // 2, core % 2
        m = {
            "xT_own": np.ascontiguousarray(x[b, h * T:(h + 1) * T, :].T),
            "xT_prev": (np.zeros((P, FC, T), bfd) if h == 0
                        else np.ascontiguousarray(
                            x[b, 0:T, :].T.reshape(FC, P, T).transpose(1, 0, 2)
                        ).astype(bfd)),
            "ones_f32": ones_f32,
            "ones_bf": ones_bf,
            "cvec": cv,
            "ident_f32": ident_f32,
            "ident_bf": ident_bf,
        }
        if "ssm" in phases:
            m["ssm_logLam"] = np.ascontiguousarray(
                np.asarray(inputs["L0_attn_log_Lam_real"]).reshape(NST, 1))
            m["ssm_LamIm"] = np.ascontiguousarray(
                np.asarray(inputs["L0_attn_Lam_imag"]).reshape(NST, 1))
            m["ssm_logdt"] = np.ascontiguousarray(
                np.asarray(inputs["L0_attn_log_dt"]).reshape(1, 1))
            m["ssm_BrT"] = np.ascontiguousarray(
                np.asarray(inputs["L0_attn_B_real"]).T.reshape(FC, P, NST)
                .transpose(1, 0, 2).reshape(P, FC * NST)).astype(bfd)
            m["ssm_BiT"] = np.ascontiguousarray(
                np.asarray(inputs["L0_attn_B_imag"]).T.reshape(FC, P, NST)
                .transpose(1, 0, 2).reshape(P, FC * NST)).astype(bfd)
            m["ssm_CC"] = np.ascontiguousarray(np.concatenate(
                [np.asarray(inputs["L0_attn_C_real"]).T,
                 np.asarray(inputs["L0_attn_C_imag"]).T], axis=0)).astype(bfd)
            m["ssm_WoutT"] = np.ascontiguousarray(
                np.asarray(inputs["L0_attn_W_out"]).T.reshape(FC, P, DM)
                .transpose(1, 0, 2).reshape(P, FC * DM)).astype(bfd)
            m["ssm_D"] = np.ascontiguousarray(
                np.asarray(inputs["L0_attn_D"]).reshape(FC, P).T)
            m["ssm_bout"] = np.ascontiguousarray(
                np.asarray(inputs["L0_attn_b_out"]).reshape(FC, P).T)
        if "moe0" in phases:
            m["moe0_gate"] = np.ascontiguousarray(np.asarray(inputs["L0_ffn_gate"]))
            m["moe0_w1"] = w_up(inputs["L0_ffn_experts_w1"])
            m["moe0_w3"] = w_up(inputs["L0_ffn_experts_w3"])
            m["moe0_w2"] = w_dn(inputs["L0_ffn_experts_w2"])
        if "attn" in phases:
            m["attn_WqT"] = w_qkv(inputs["L1_attn_Wq"], h)
            m["attn_WkT"] = w_qkv(inputs["L1_attn_Wk"], h)
            m["attn_WvT"] = w_qkv(inputs["L1_attn_Wv"], h)
            m["attn_WoT"] = np.ascontiguousarray(
                np.asarray(inputs["L1_attn_Wo"]).T.reshape(FC, P, DM)
                .transpose(1, 0, 2).reshape(P, FC * DM)).astype(bfd)
            m["rope_cos"] = rope_cos
            m["rope_sin"] = rope_sin
            m["attn_masks"] = attn_masks
            m["tok0"] = np.array([[h * T]], dtype=np.int32)
        if "moe1" in phases:
            m["moe1_gate"] = np.ascontiguousarray(np.asarray(inputs["L1_ffn_Wr"]))
            m["moe1_w1"] = w_up(inputs["L1_ffn_Wg"])
            m["moe1_w3"] = w_up(inputs["L1_ffn_Wu"])
            m["moe1_w2"] = w_dn(inputs["L1_ffn_Wd"])
        maps.append(m)
    return maps


def kernel(**inputs):
    phases = os.environ.get("KPHASES", PHASES_DEFAULT).split(",")
    maps = prep_inputs(inputs, phases)
    nc = build_program(phases)
    res = run_bass_kernel_spmd(nc, maps, core_ids=list(range(8)))
    out = np.empty((4, FT, DM), dtype=np.float32)
    for core in range(8):
        b, h = core // 2, core % 2
        out[b, h * T:(h + 1) * T, :] = res.results[core]["outT"].T
    return out
